# version 18
# speedup vs baseline: 1.5008x; 1.0778x over previous
import contextlib
import numpy as np
import ml_dtypes
import concourse.bass as bass
import concourse.mybir as mybir
from concourse.bass_utils import run_bass_kernel_spmd
from concourse.expressions import smin
import bass_rust as _bass_rust
from concourse.library_config import all_libraries, standard as _std_lib


def _finalize_libs(nc):
    inst_map = {}
    for lib in all_libraries:
        for it in lib.instructions:
            inst_map[it] = inst_map.get(it, 0) | (1 << lib.index)
    _bass_rust.insert_library_loads(nc, inst_map, len(all_libraries), _std_lib.index)
    mybir.codegen_inst_isa_subclasses(nc)

F32 = mybir.dt.float32
BF16 = mybir.dt.bfloat16
I16 = mybir.dt.int16
I32 = mybir.dt.int32
AF = mybir.ActivationFunctionType
ALU = mybir.AluOpType
AX = mybir.AxisListType

B, S, D = 4, 4096, 1024
HALF = S // 2
NT = S // 128
NTH = HALF // 128
NDC = D // 128
HEADS, KV_HEADS, HD = 16, 4, 64
E_TOT, E_MINE = 8, 4
MOE_HID = 4096
CAP = int(S * 2.0 / E_TOT) + 1
CAP_PAD = 1152
L1_HID = 3072
DEQ = float(np.float32(1.0 / 127.0))
EXP_SHIFT = 15.0
BISECT_ITERS = 40

SIGMA = [0] * HEADS
_ev = iter([0, 2, 4, 6, 8, 10, 12, 14])
_od = iter([1, 3, 5, 7, 9, 11, 13, 15])
for _h in range(HEADS):
    SIGMA[_h] = next(_ev) if (_h // 4) % 2 == 0 else next(_od)
SLOT_TO_HEAD = [0] * HEADS
for _h2, _s2 in enumerate(SIGMA):
    SLOT_TO_HEAD[_s2] = _h2

ENGINES = ("sync", "scalar", "vector", "tensor", "gpsimd")

_DRAIN_OPS = frozenset({
    "tensor_tensor", "tensor_add", "tensor_sub", "tensor_mul", "tensor_max",
    "tensor_scalar", "tensor_single_scalar", "tensor_scalar_add",
    "tensor_scalar_sub", "tensor_scalar_mul", "tensor_scalar_max",
    "tensor_scalar_min", "tensor_copy", "tensor_reduce", "reduce_max",
    "reduce_sum", "scalar_tensor_tensor", "tensor_tensor_scan", "memset",
    "memzero", "reciprocal", "select", "copy_predicated", "activation",
    "copy", "add", "mul", "square", "sqrt", "sign", "iota",
    "partition_broadcast", "partition_all_reduce", "sparse_gather",
    "kth_largest", "dma_gather", "dma_scatter_add", "value_load",
    "transpose", "stream_shuffle",
})


class _DrainingEngine:

    def __init__(self, eng):
        object.__setattr__(self, "_e", eng)
        object.__setattr__(self, "_dirty", False)
        object.__setattr__(self, "_skip", False)

    def __getattr__(self, name):
        attr = getattr(self._e, name)
        if name in _DRAIN_OPS and callable(attr):
            def wrapped(*a, **kw):
                if (object.__getattribute__(self, "_dirty")
                        and not object.__getattribute__(self, "_skip")):
                    self._e.drain(fusable=True)
                object.__setattr__(self, "_dirty", True)
                return attr(*a, **kw)
            return wrapped
        return attr


class Sched:

    def __init__(self, nc, ctx):
        self.nc = nc
        self.items = []
        self.counts = {}
        self.handles = {}
        self.alias = {}
        self._ctx = ctx

    DEDICATED = frozenset({
        "afd",
        "bob",
        "atl0",
        "atl1",
        "ats0",
        "ats1",
        "b1l",
        "b2l",
        "bar",
        "bwol",
        "bxl0",
        "bxl1",
        "bxl2",
        "cs0",
        "cs0b",
        "csl",
        "cst",
        "done",
        "ifl",
        "isc",
        "ixl",
        "ld_gpsimd",
        "ld_scalar",
        "ld_sync",
        "mgg",
        "mgl0",
        "mgl1",
        "msc",
        "os0",
        "os1",
        "rml",
        "rwl",
        "scg",
        "sct_d",
        "sfd",
        "sts",
        "tkl",
        "tp0",
        "tp1",
        "tp2",
        "tp3",
        "w1l",
        "w2l",
        "wc",
        "wdl0",
        "wdl1",
        "wgl0",
        "wgl1",
        "wkl",
        "wo_d0",
        "wo_d1",
        "wol",
        "wql",
        "wul0",
        "wul1",
        "wvl",
        "x1l0",
        "x1l1",
        "x1s0",
        "x1s1",
        "xl0",
        "xl1",
        "xl2",
        "xns",
        "xns_d",
        "ya0",
        "ya1",
        "zacc",
        "zrow",
    })
    POOL = 30

    def _phys(self, name):
        if name in self.DEDICATED or (name.startswith("sp") and name[2:].isdigit()):
            return name
        if name not in self.alias:
            self.alias[name] = "sp%d" % (len(self.alias) % self.POOL)
        return self.alias[name]

    def sem(self, name):
        name = self._phys(name)
        if name not in self.handles:
            self.handles[name] = self._ctx.enter_context(self.nc.semaphore(name))
            self.counts[name] = 0
        return self.handles[name]

    def val(self, name):
        name = self._phys(name)
        self.sem(name)
        return self.counts[name]

    def emit(self, eng, fn, waits=(), inc=None, nodrain=False):
        ws = []
        for w in waits:
            if w is None:
                continue
            (nm, v) = w
            nm = self._phys(nm)
            self.sem(nm)
            if v > 0:
                ws.append((nm, v))
        incs = []
        if inc is not None:
            nm, n = inc
            nm = self._phys(nm)
            self.sem(nm)
            self.counts[nm] += n
            incs.append((nm, n))
        self.items.append((eng, fn, ws, incs, nodrain))
        return (incs[0][0], self.counts[incs[0][0]]) if inc else None

    def barrier(self):
        nm = "bar"
        self.sem(nm)
        target = self.counts[nm] + len(ENGINES)
        for e in ENGINES:
            self.counts[nm] += 1
            self.items.append((e, "nop", [], [(nm, 1)], False))
        for e in ENGINES:
            self.items.append((e, "wait", [(nm, target)], [], False))

    def dma(self, q, out, in_, waits=(), sem=None):
        nm = sem if sem is not None else ("ld_" + q)
        return self.emit(q, lambda e, o=out, i=in_: e.dma_start(out=o, in_=i),
                         waits=waits, inc=(nm, 16))

    def run(self, engname, eng):
        if engname in ("vector", "scalar", "gpsimd"):
            eng = _DrainingEngine(eng)
        for (e, fn, waits, incs, nodrain) in self.items:
            if e != engname:
                continue
            if isinstance(eng, _DrainingEngine):
                object.__setattr__(eng, "_skip", nodrain)
            for (nm, v) in waits:
                eng.wait_ge(self.handles[nm], v)
            if fn == "nop":
                ins = eng.engine_nop() if hasattr(eng, "engine_nop") else eng.nop()
            elif fn == "wait":
                continue
            else:
                ins = fn(eng)
            if incs:
                if isinstance(ins, (list, tuple)):
                    ins = ins[-1]
                assert ins is not None
                for (nm, n) in incs:
                    ins.then_inc(self.handles[nm], n)

    def finish(self):
        nc = self.nc
        with nc.Block() as block:
            @block.sync
            def _(e):
                self.run("sync", e)

            @block.scalar
            def _(e):
                self.run("scalar", e)

            @block.vector
            def _(e):
                self.run("vector", e)

            @block.tensor
            def _(e):
                self.run("tensor", e)

            @block.gpsimd
            def _(e):
                self.run("gpsimd", e)


def _reshape(ap, shape):
    if len(shape) == 1:
        return ap
    letters = "abcdefg"[: len(shape)]
    pat = "p ({}) -> p {}".format(" ".join(letters), " ".join(letters))
    return ap.rearrange(pat, **{letters[i]: shape[i] for i in range(len(shape))})


class Arena:
    def __init__(self, nc, name, n_elems, dtype):
        self.t = nc.alloc_sbuf_tensor(name, [128, n_elems], dtype)
        self.n = n_elems
        self.off = 0
        self.marks = []
        self.peak = 0

    def alloc(self, *shape):
        n = int(np.prod(shape))
        ap = self.t.ap()[:, self.off:self.off + n]
        self.off += n
        self.peak = max(self.peak, self.off)
        assert self.off <= self.n, f"arena overflow {self.off} > {self.n}"
        return _reshape(ap, shape)

    def push(self):
        self.marks.append(self.off)

    def pop(self):
        self.off = self.marks.pop()


def _nopi(eng):
    return eng.engine_nop() if hasattr(eng, "engine_nop") else eng.nop()



def host_consts():
    inv = 1.0 / (10000.0 ** (np.arange(0, HD, 2, dtype=np.float64) / HD))
    pos = np.arange(S, dtype=np.float64)
    fr = pos[:, None] * inv[None, :]
    fr = np.concatenate([fr, fr], axis=1)
    cos64 = np.concatenate([np.cos(fr).T] * 2, axis=0)
    sin64 = np.concatenate([np.sin(fr).T] * 2, axis=0)
    cos2 = np.ascontiguousarray(cos64.astype(np.float32))
    sin2 = np.ascontiguousarray(sin64.astype(np.float32))
    coslo = np.ascontiguousarray((cos64 - cos2.astype(np.float64)).astype(np.float32))
    sinlo = np.ascontiguousarray((sin64 - sin2.astype(np.float64)).astype(np.float32))

    R = np.zeros((64, 64), dtype=np.float32)
    for i in range(32):
        R[i, i + 32] = -1.0
        R[i + 32, i] = 1.0
    R2 = np.zeros((128, 128), dtype=np.float32)
    R2[:64, :64] = R
    R2[64:, 64:] = R
    rotT = np.ascontiguousarray(R2.T)

    idn = np.eye(128, dtype=np.float32)
    bde = np.zeros((128, E_TOT), dtype=np.float32)
    for p in range(128):
        bde[p, p // 16] = 1.0
    bdeT = np.ascontiguousarray(bde.T)
    return cos2, sin2, coslo, sinlo, rotT, idn, bde, bdeT



def build_fused(debug_taps=(), phases=("p3", "p4", "p4b", "p5")):
    nc = bass.Bass("TRN2", target_bir_lowering=False, debug=False)

    x_d = nc.dram_tensor("x", [S, D], F32, kind="ExternalInput").ap()
    wq_d = nc.dram_tensor("wq", [D, D], F32, kind="ExternalInput").ap()
    wkv_d = nc.dram_tensor("wkv", [D, 512], F32, kind="ExternalInput").ap()
    wo_d = nc.dram_tensor("wo", [D, D], F32, kind="ExternalInput").ap()
    bo_d = nc.dram_tensor("bo", [1, D], F32, kind="ExternalInput").ap()
    rwt_d = nc.dram_tensor("rwt", [D, E_TOT], F32, kind="ExternalInput").ap()
    w1_d = nc.dram_tensor("w1", [E_MINE, D, MOE_HID], F32, kind="ExternalInput").ap()
    b1_d = nc.dram_tensor("b1", [E_MINE, 128, MOE_HID // 128], F32, kind="ExternalInput").ap()
    w2_d = nc.dram_tensor("w2", [E_MINE, MOE_HID, D], F32, kind="ExternalInput").ap()
    b2_d = nc.dram_tensor("b2", [E_MINE, 1, D], F32, kind="ExternalInput").ap()
    cos_d = nc.dram_tensor("cos", [128, S], F32, kind="ExternalInput").ap()
    sin_d = nc.dram_tensor("sin", [128, S], F32, kind="ExternalInput").ap()
    coslo_d = nc.dram_tensor("coslo", [128, S], F32, kind="ExternalInput").ap()
    sinlo_d = nc.dram_tensor("sinlo", [128, S], F32, kind="ExternalInput").ap()
    rot_d = nc.dram_tensor("rot", [128, 128], F32, kind="ExternalInput").ap()
    idn_d = nc.dram_tensor("idn", [128, 128], F32, kind="ExternalInput").ap()
    bde_d = nc.dram_tensor("bde", [128, E_TOT], F32, kind="ExternalInput").ap()
    bdeT_d = nc.dram_tensor("bdeT", [E_TOT, 128], F32, kind="ExternalInput").ap()

    dattn_d = nc.dram_tensor("dattn", [HALF, D], F32, kind="Internal").ap()
    macc_d = nc.dram_tensor("macc", [S + 1, D], F32, kind="Internal").ap()
    xn_tm_d = nc.dram_tensor("xn_tm", [S + 1, D], BF16, kind="Internal").ap()
    sc_tm_d = nc.dram_tensor("sc_tm", [S + 1, 64], F32, kind="Internal").ap()
    scf_d = nc.dram_tensor("scf_d", [E_TOT, S], F32, kind="Internal").ap()
    idxd_d = nc.dram_tensor("idxd", [E_MINE, 10496], I32, kind="Internal").ap()
    utri_d = nc.dram_tensor("utri", [128, 128], F32, kind="ExternalInput").ap()
    onef_d = nc.dram_tensor("onef", [128, 128], F32, kind="ExternalInput").ap()
    tokid_d = nc.dram_tensor("tokid_in", [128, NT], I32, kind="ExternalInput").ap()
    xnat_d = nc.dram_tensor("xnat", [S, D], F32, kind="ExternalInput").ap()
    rowmap_d = nc.dram_tensor("rowmap", [128, NT], I32, kind="ExternalInput").ap()
    wq1_d = nc.dram_tensor("wq1", [D, D], F32, kind="ExternalInput").ap()
    wk1_d = nc.dram_tensor("wk1", [D, D], F32, kind="ExternalInput").ap()
    wv1_d = nc.dram_tensor("wv1", [D, D], F32, kind="ExternalInput").ap()
    wo1_d = nc.dram_tensor("wo1", [D, D], F32, kind="ExternalInput").ap()
    wg_d = nc.dram_tensor("wg", [D, L1_HID], I32, kind="ExternalInput").ap()
    wu_d = nc.dram_tensor("wu", [D, L1_HID], I32, kind="ExternalInput").ap()
    wd_d = nc.dram_tensor("wd", [L1_HID, D], I32, kind="ExternalInput").ap()
    tri_d = nc.dram_tensor("tri", [128, 128], F32, kind="ExternalInput").ap()
    pmask_d = nc.dram_tensor("pmask", [128, NT], F32, kind="ExternalInput").ap()
    out_d = nc.dram_tensor("out", [HALF, D], F32, kind="ExternalOutput").ap()
    x1_d = nc.dram_tensor("x1sc", [HALF, D], F32, kind="Internal").ap()
    attn1_d = nc.dram_tensor("attn1sp", [HALF, D], BF16, kind="Internal").ap()
    maccsum_d = nc.dram_tensor("maccsum", [S, D], F32, kind="Internal").ap()
    dattncat_d = nc.dram_tensor("dattncat", [S, D], F32, kind="Internal").ap()

    taps = {}

    def tap(name, shape, dtype=F32):
        if name in debug_taps:
            taps[name] = nc.dram_tensor("tap_" + name, list(shape), dtype,
                                        kind="ExternalOutput").ap()
            return taps[name]
        return None

    ctx = contextlib.ExitStack()
    s = Sched(nc, ctx)

    with ctx:
        bfA = Arena(nc, "bfA", 74300, BF16)
        fpA = Arena(nc, "fpA", 10400, F32)
        A = nc.alloc_sbuf_tensor

        idn = A("idn_s", [128, 128], F32)
        idnb = A("idnb", [128, 128], BF16)
        rott = A("rott", [128, 128], BF16)
        ones8 = A("ones8", [8, 1], F32)
        onesr = A("onesr", [1, E_TOT], F32)
        bde = A("bde_s", [128, E_TOT], F32)
        bdeT = A("bdeT_s", [E_TOT, 128], F32)
        sc_g = A("sc_g", [128, 256], F32)
        lo_t = A("lo_t", [8, 6], F32)
        thr = A("thr", [128, 1], F32)
        selv = A("selv", [128, 256], F32)
        mt_f = A("mt_f", [8, 128], F32)
        mtb = A("mtb", [128, NT, E_TOT], BF16)
        ctm = A("ctm", [128, NT, E_TOT], F32)
        sidx = A("sidx", [128, NT, E_TOT], I32)
        tokid = A("tokid", [128, NT], I32)
        stmp = A("stmp", [128, 2, E_TOT], F32)
        itmp = A("itmp", [128, 3, E_TOT], I32)
        idx_sb = A("idx_sb", [128, 9], I32)
        fill_t = A("fill_t", [128, 82], I32)
        utri = A("utri_s", [128, 128], BF16)
        onet = A("onet_s", [128, 128], BF16)
        zb = A("zb", [1, D], BF16)
        zf = A("zf", [1, 64], F32)
        deq_t = A("deq_t", [128, 2], F32)
        tri = A("tri_s", [128, 128], F32)
        pmask = A("pmask_s", [128, NT], F32)
        onesb = A("onesb", [128, 1], BF16)
        S_sb = A("S_sb", [128, NDC, 65], F32)
        S_bf = A("S_bf", [128, NDC, 65], BF16)
        rowmap = A("rowmap_s", [128, NT], I32)
        ln_m = A("ln_m", [128, 8], F32)
        r_sb = A("r_sb", [128, 1], F32)
        ones64 = A("ones64", [128, 64], F32)
        nshift = A("nshift", [128, 1], F32)
        gt_t = A("gt_t", [128, 9, 64], F32)
        xg_tm = A("xg_tm", [128, 2, D], BF16)
        b1_sb = A("b1_sb", [128, E_MINE, MOE_HID // 128], F32)

        AP_ = nc.alloc_psum_tensor
        psA = AP_("psA", [128, 512], F32)
        psB = AP_("psB", [128, 512], F32)
        psC = AP_("psC", [128, 512], F32)
        psD = AP_("psD", [128, 512], F32)
        psTa = AP_("psTa", [128, 1024], BF16)
        psTb = AP_("psTb", [128, 1024], BF16)
        psY = AP_("psY", [128, 1024], F32)

        dma = s.dma
        from concourse.bass_core_rust import IndirectOffsetOnAxis as _IOAx

        bfA.push()
        kfm = bfA.alloc(2, S)
        vtm = bfA.alloc(NT, 4 * 65)
        qfm = bfA.alloc(NDC, HALF)
        at_fm = bfA.alloc(NDC, HALF)
        fpA.push()
        zero_t = fpA.alloc(D)
        zero_t2 = _reshape(zero_t, (1, D))

        dma("sync", idn[:], idn_d, sem="cs0")
        dma("sync", bde[:], bde_d, sem="cs0")
        cs0v = dma("sync", bdeT[:], bdeT_d, sem="cs0")
        dma("gpsimd", idnb[:], idn_d, sem="cst")
        dma("gpsimd", rott[:], rot_d, sem="cst")
        for e in range(E_MINE):
            c0 = dma("sync", b1_sb[:, e, :], b1_d[e], sem="b1l")
        s.emit("vector", lambda e: e.memset(zero_t, 0.0), inc=("mi", 1))
        s.emit("vector", lambda e: e.memset(ones8[:], 1.0), inc=("mi", 1))
        s.emit("vector", lambda e: e.memset(onesr[:], 1.0), inc=("mi", 1))
        s.emit("vector", lambda e: e.memset(nshift[:], -EXP_SHIFT), inc=("mi", 1))
        s.emit("vector", lambda e: e.memset(fill_t[:], S), inc=("mi", 1))
        s.emit("vector", lambda e: e.memset(zb[:], 0.0), inc=("mi", 1))
        s.emit("vector", lambda e: e.memset(zf[:], 0.0), inc=("mi", 1))
        dma("gpsimd", utri[:], utri_d, sem="cst")
        cstv2 = dma("gpsimd", onet[:], onef_d, sem="cst")
        cstv = cstv2
        tiv = dma("sync", tokid[:], tokid_d, sem="tkl")
        zbv = ("mi", s.val("mi"))
        dma("scalar", xn_tm_d[S:S + 1, :], zb[:], sem="zrow", waits=[zbv])
        dma("scalar", sc_tm_d[S:S + 1, :], zf[:], sem="zrow", waits=[zbv])
        for kv2 in range(4):
            s.emit("vector", lambda e, kv2=kv2: e.memset(
                vtm[:, :, 65 * kv2 + 64:65 * kv2 + 65], 1.0), inc=("mi", 1))
        s.emit("vector", lambda e: e.memset(ones64[:], 1.0), inc=("mi", 1))
        zv = ("mi", s.val("mi"))
        for t in range(NT):
            dma("scalar", macc_d[128 * t:128 * (t + 1), :], zero_t,
                waits=[zv], sem="zacc")
        dma("scalar", macc_d[S:S + 1, :], zero_t[0:1, :], waits=[zv], sem="zacc")
        fpA.pop()

        fpA.push()
        scf = fpA.alloc(S)[0:8, :]
        bfA.push()
        fpA.push()
        xn_fm = bfA.alloc(NDC, HALF)
        xn_t = bfA.alloc(D)
        wq_sb = bfA.alloc(NDC, 128)
        wkvk_sb = bfA.alloc(NDC, 128)
        wkvv_sb = bfA.alloc(NDC, 256)
        raw_t = bfA.alloc(512)
        cs_t = bfA.alloc(2, 512)
        xt = fpA.alloc(2, D)
        sq_t = fpA.alloc(D)
        rpf = fpA.alloc(2, 512)
        xn_f = fpA.alloc(D)
        xnf_col = fpA.alloc(2, 128)
        rwtf = fpA.alloc(NDC, E_TOT)

        wvc = dma("gpsimd", wkvv_sb, wkv_d[:, 256:512].rearrange("(c p) o -> p c o", p=128),
                  sem="wc")
        rwv = dma("sync", rwtf, rwt_d.rearrange("(c p) o -> p c o", p=128),
                  sem="rwl")

        for half in range(2):
            col0 = HALF * half
            for tt in range(NTH):
                t = NTH * half + tt
                buf = tt % 2
                xin = dma("sync", xt[:, buf, :], x_d[128 * t:128 * (t + 1), :],
                          sem="xl" + str(buf),
                          waits=[("lnD", s.val("lnD") - 1),
                                 ("zacc", (NT + 1) * 16) if (half == 0 and tt == 0) else None])
                s.emit("vector", lambda e, b=buf: e.reduce_sum(
                    ln_m[:, 0:1], xt[:, b, :], axis=AX.X), waits=[xin])
                sv = s.emit("scalar", lambda e, b=buf: e.activation(
                    out=sq_t, in_=xt[:, b, :], func=AF.Square,
                    accum_out=ln_m[:, 1:2]),
                    waits=[xin, ("lnA", s.val("lnA"))], inc=("lnB", 1))

                def ln_stats(e):
                    e.tensor_scalar(ln_m[:, 2:3], ln_m[:, 0:1], 1.0 / D, None, ALU.mult)
                    e.tensor_scalar(ln_m[:, 3:4], ln_m[:, 1:2], 1.0 / D, None, ALU.mult)
                    e.tensor_tensor(ln_m[:, 5:6], ln_m[:, 2:3], ln_m[:, 2:3], ALU.mult)
                    e.tensor_tensor(ln_m[:, 6:7], ln_m[:, 3:4], ln_m[:, 5:6], ALU.subtract)
                    return e.tensor_scalar(ln_m[:, 6:7], ln_m[:, 6:7], 1e-5, None, ALU.add)
                av = s.emit("vector", ln_stats, waits=[("lnB", s.val("lnB"))],
                            inc=("lnA", 1))
                sv2 = s.emit("scalar", lambda e: e.activation(
                    out=ln_m[:, 7:8], in_=ln_m[:, 6:7], func=AF.Sqrt),
                    waits=[av, ("lnD", s.val("lnD"))], inc=("lnC", 1))

                def ln_norm(e, b=buf):
                    e.reciprocal(ln_m[:, 7:8], ln_m[:, 7:8])
                    e.tensor_scalar(xn_f, xt[:, b, :],
                                    ln_m[:, 2:3], ln_m[:, 7:8],
                                    ALU.subtract, ALU.mult)
                    return e.tensor_copy(xn_t, xn_f)
                lnv = s.emit("vector", ln_norm,
                             waits=[sv2, ("xns", s.val("xns")),
                                    ("tpe", s.val("tpe")),
                                    ("rtl", s.val("rtl"))], inc=("lnD", 1))
                s.emit("gpsimd", lambda e, t=t: e.indirect_dma_start(
                    xn_tm_d, _IOAx(tokid[:, t:t + 1], 0), xn_t, None),
                    waits=[lnv, tiv], inc=("xns_d", 16))
                s.emit("scalar", lambda e: _nopi(e),
                       waits=[("xns_d", s.val("xns_d"))], inc=("xns", 1))
                for dc in range(NDC):
                    pst = [psA, psB][dc % 2]
                    tv = s.emit("tensor", lambda e, dc=dc, p=pst: e.transpose(
                        p[:, 0:128], xn_f[:, 128 * dc:128 * (dc + 1)], idn[:]),
                        waits=[lnv, ("tpe", s.val("tpe") - 1), cs0v],
                        inc=("tpm", 1))

                    def evac2(e, dc=dc, tt=tt, p=pst):
                        e.tensor_copy(xn_fm[:, dc, 128 * tt:128 * (tt + 1)],
                                      p[:, 0:128])
                        return e.tensor_copy(xnf_col[:, dc % 2, :], p[:, 0:128])
                    s.emit("vector", evac2, waits=[tv, ("rtm", s.val("rtm"))],
                           inc=("tpe", 1))
                    s.emit("tensor", lambda e, dc=dc, t=t: e.matmul(
                        psD[0:8, 0:128], rwtf[:, dc, :], xnf_col[:, dc % 2, :],
                        start=(dc == 0), stop=(dc == NDC - 1)),
                        waits=[("tpe", s.val("tpe")), rwv] +
                              ([("rtl", s.val("rtl"))] if dc == 0 else []),
                        inc=("rtm", 1))
                s.emit("vector", lambda e, t=t: e.tensor_copy(
                    scf[:, 128 * t:128 * (t + 1)], psD[0:8, 0:128]),
                    waits=[("rtm", s.val("rtm"))], inc=("rtl", 1))
            lndone = ("tpe", s.val("tpe"))

            for oc in range(2):
                wvk = dma("gpsimd", wkvk_sb,
                          wkv_d[:, 128 * oc:128 * (oc + 1)].rearrange("(c p) o -> p c o", p=128),
                          sem="wkl",
                          waits=[("kpm", s.val("kpm")), ("qpm", s.val("qpm"))])
                for g in range(4):
                    mv = s.emit("tensor", lambda e, oc=oc, g=g: [e.matmul(
                        psB[:], wkvk_sb[:, dc, :],
                        xn_fm[:, dc, 512 * g:512 * (g + 1)],
                        start=(dc == 0), stop=(dc == NDC - 1)) for dc in range(NDC)][-1],
                        waits=[lndone, wvk, ("kpe", s.val("kpe")),
                               ("rtl", s.val("rtl"))], inc=("kpm", 1))
                    kv_ = s.emit("vector", lambda e, g=g: e.tensor_copy(
                        raw_t, psB[:]),
                        waits=[mv, ("rpe", s.val("rpe"))], inc=("kpe", 1))
                    dma("gpsimd", cs_t[:, 0, :],
                        cos_d[:, col0 + 512 * g:col0 + 512 * (g + 1)],
                        sem="csl", waits=[("rpe", s.val("rpe"))])
                    cvh = dma("gpsimd", cs_t[:, 1, :],
                              sin_d[:, col0 + 512 * g:col0 + 512 * (g + 1)], sem="csl")
                    s.emit("gpsimd", lambda e, g=g, col0=col0: e.dma_start(
                        out=cs_t[:, 0, :],
                        in_=coslo_d[:, col0 + 512 * g:col0 + 512 * (g + 1)],
                        accum_op=ALU.add), waits=[cvh], inc=("csl", 16))
                    cv2 = s.emit("gpsimd", lambda e, g=g, col0=col0: e.dma_start(
                        out=cs_t[:, 1, :],
                        in_=sinlo_d[:, col0 + 512 * g:col0 + 512 * (g + 1)],
                        accum_op=ALU.add), waits=[cvh], inc=("csl", 16))
                    rv = s.emit("tensor", lambda e, g=g: e.matmul(
                        psC[:], rott[:], raw_t, start=True, stop=True),
                        waits=[kv_, ("rpe", s.val("rpe")), cstv],
                        inc=("rpm", 1))

                    def rope_k(e, g=g, oc=oc, col0=col0):
                        lo = col0 + 512 * g
                        e.tensor_tensor(rpf[:, 0, :], psC[:], cs_t[:, 1, :], ALU.mult)
                        e.tensor_tensor(rpf[:, 1, :], raw_t, cs_t[:, 0, :], ALU.mult)
                        return e.tensor_tensor(kfm[:, oc, lo:lo + 512], rpf[:, 0, :],
                                               rpf[:, 1, :], ALU.add)
                    s.emit("vector", rope_k, waits=[rv, cv2], inc=("rpe", 1))

            for tt in range(NTH):
                t = NTH * half + tt
                mv = s.emit("tensor", lambda e, tt=tt: [e.matmul(
                    psC[:, 0:256], xn_fm[:, dc, 128 * tt:128 * (tt + 1)], wkvv_sb[:, dc, :],
                    start=(dc == 0), stop=(dc == NDC - 1)) for dc in range(NDC)][-1],
                    waits=[lndone, wvc, ("vpe", s.val("vpe")),
                           ("rpe", s.val("rpe"))], inc=("vpm", 1))

                def v_evac(e, t=t):
                    ins = None
                    for kv2 in range(4):
                        ins = e.tensor_copy(vtm[:, t, 65 * kv2:65 * kv2 + 64],
                                            psC[:, 64 * kv2:64 * (kv2 + 1)])
                    return ins
                s.emit("vector", v_evac, waits=[mv, zv], inc=("vpe", 1))

            if half == 0:
                for oc in range(NDC):
                    wvq = dma("gpsimd", wq_sb,
                              wq_d[:, 128 * oc:128 * (oc + 1)].rearrange("(c p) o -> p c o", p=128),
                              sem="wkl",
                              waits=[("qpm", s.val("qpm")), ("kpm", s.val("kpm"))])
                    for g in range(4):
                        mv = s.emit("tensor", lambda e, oc=oc, g=g: [e.matmul(
                            psD[:], wq_sb[:, dc, :],
                            xn_fm[:, dc, 512 * g:512 * (g + 1)],
                            start=(dc == 0), stop=(dc == NDC - 1)) for dc in range(NDC)][-1],
                            waits=[lndone, wvq, ("qpe", s.val("qpe")),
                                   ("rtl", s.val("rtl"))], inc=("qpm", 1))
                        qv_ = s.emit("vector", lambda e, g=g: e.tensor_copy(
                            raw_t, psD[:]),
                            waits=[mv, ("rpe", s.val("rpe"))], inc=("qpe", 1))
                        dma("gpsimd", cs_t[:, 0, :], cos_d[:, 512 * g:512 * (g + 1)],
                            sem="csl", waits=[("rpe", s.val("rpe"))])
                        cvh = dma("gpsimd", cs_t[:, 1, :], sin_d[:, 512 * g:512 * (g + 1)],
                                  sem="csl")
                        s.emit("gpsimd", lambda e, g=g: e.dma_start(
                            out=cs_t[:, 0, :], in_=coslo_d[:, 512 * g:512 * (g + 1)],
                            accum_op=ALU.add), waits=[cvh], inc=("csl", 16))
                        cv2 = s.emit("gpsimd", lambda e, g=g: e.dma_start(
                            out=cs_t[:, 1, :], in_=sinlo_d[:, 512 * g:512 * (g + 1)],
                            accum_op=ALU.add), waits=[cvh], inc=("csl", 16))
                        rv = s.emit("tensor", lambda e, g=g: e.matmul(
                            psC[:], rott[:], raw_t, start=True, stop=True),
                            waits=[qv_, ("rpe", s.val("rpe")),
                                   ("vpe", s.val("vpe"))], inc=("rpm", 1))

                        def rope_q(e, g=g, oc=oc):
                            e.tensor_tensor(rpf[:, 0, :], psC[:], cs_t[:, 1, :], ALU.mult)
                            e.tensor_tensor(rpf[:, 1, :], raw_t, cs_t[:, 0, :], ALU.mult)
                            return e.tensor_tensor(qfm[:, oc, 512 * g:512 * (g + 1)],
                                                   rpf[:, 0, :], rpf[:, 1, :], ALU.add)
                        s.emit("vector", rope_q, waits=[rv, cv2], inc=("rpe", 1))
            s.barrier()
        bfA.pop()
        fpA.pop()

        bv_ = ("bar", s.val("bar"))
        if tap("kfm", (128, 2, S), BF16) is not None:
            dma("sync", taps["kfm"], kfm, sem="tp0", waits=[bv_])
        if tap("qfm", (128, NDC, HALF), BF16) is not None:
            dma("sync", taps["qfm"], qfm, sem="tp0", waits=[bv_])
        if tap("vtm", (128, NT, 260), BF16) is not None:
            dma("sync", taps["vtm"], vtm, sem="tp0", waits=[bv_])
        if tap("logits", (8, S)) is not None:
            dma("sync", taps["logits"], scf[:], sem="tp0", waits=[bv_])

        if "p3" not in phases:
            fin0 = [("xns_d", s.val("xns_d"))]
            for nm in ("tp0",):
                if s.val(nm):
                    fin0.append((nm, s.val(nm)))
            s.emit("sync", lambda e: _nopi(e), waits=fin0, inc=("done", 1))
            s.finish()
            _finalize_libs(nc)
            return nc, taps
        fpA.push()
        o_sb = fpA.alloc(512)
        o_sb2 = _reshape(o_sb, (1, 512))

        sxv = s.emit("scalar", lambda e: e.activation(out=scf[:], in_=scf[:],
                                                      func=AF.Exp),
                     waits=[("tp0", s.val("tp0"))], inc=("sxm", 1))
        for g in range(8):
            mv = s.emit("tensor", lambda e, g=g: e.matmul(
                psA[0:1, :], ones8[:], scf[:, 512 * g:512 * (g + 1)],
                start=True, stop=True), waits=[sxv, ("sce", s.val("sce"))],
                inc=("zsm", 1))
            rv = s.emit("vector", lambda e: e.reciprocal(o_sb2[0:1, :], psA[0:1, :]),
                        waits=[mv], inc=("zre", 1))
            bv = s.emit("tensor", lambda e: e.matmul(
                psB[0:8, :], onesr[:], o_sb2[0:1, :], start=True, stop=True),
                waits=[rv], inc=("zbm", 1))
            s.emit("vector", lambda e, g=g: e.tensor_tensor(
                scf[:, 512 * g:512 * (g + 1)], scf[:, 512 * g:512 * (g + 1)],
                psB[0:8, :], ALU.mult), waits=[bv], inc=("sce", 1))
        scdone = ("sce", s.val("sce"))

        if tap("scf", (8, S)) is not None:
            dma("sync", taps["scf"], scf[:], sem="tp1", waits=[scdone])

        s.emit("vector", lambda e: e.memset(o_sb[:, 8:64], 0.0), inc=("stz", 1))
        for t in range(NT):
            tv = s.emit("tensor", lambda e, t=t: e.transpose(
                psC[0:128, 0:8], scf[:, 128 * t:128 * (t + 1)], idn[0:8, 0:8]),
                waits=[scdone, ("ste", s.val("ste")), cs0v], inc=("stm", 1))
            cv = s.emit("vector", lambda e: e.tensor_copy(o_sb[:, 0:8], psC[0:128, 0:8]),
                        waits=[tv, ("sts", s.val("sts")), ("stz", 1)],
                        inc=("ste", 1))
            s.emit("gpsimd", lambda e, t=t: e.indirect_dma_start(
                sc_tm_d, _IOAx(tokid[:, t:t + 1], 0), o_sb[:, 0:64], None),
                waits=[cv, tiv], inc=("sct_d", 16))
            s.emit("scalar", lambda e: _nopi(e),
                   waits=[("sct_d", s.val("sct_d"))], inc=("sts", 1))
        scfd = dma("sync", scf_d, scf[:], waits=[scdone], sem="sfd")
        for e in range(E_TOT):
            dma("sync", sc_g[16 * e:16 * e + 16, :],
                scf_d[e].rearrange("(p f) -> p f", p=16), waits=[scfd], sem="scg")
        scgv = ("scg", s.val("scg"))

        def bis_init(e):
            e.memset(lo_t[:, 0:1], 0.0)
            e.memset(lo_t[:, 1:2], 1.0)
            e.memset(lo_t[:, 2:3], 0.5)
            return e.memset(thr[:], 0.5)
        s.emit("vector", bis_init, waits=[scgv], inc=("bis", 1))
        for it in range(BISECT_ITERS):
            def cnt(e):
                e.tensor_scalar(selv[:], sc_g[:], thr[:], None, ALU.is_gt)
                return e.reduce_sum(ln_m[:, 0:1], selv[:], axis=AX.X)
            cv = s.emit("vector", cnt, waits=[("bis", s.val("bis"))], inc=("bis", 1))
            mv = s.emit("tensor", lambda e: e.matmul(
                psA[0:8, 0:1], bde[:], ln_m[:, 0:1], start=True, stop=True),
                waits=[cv, cs0v], inc=("bim", 1))

            def upd(e):
                e.tensor_scalar(lo_t[:, 3:4], psA[0:8, 0:1], float(CAP) - 0.5,
                                None, ALU.is_gt)
                e.tensor_tensor(lo_t[:, 4:5], lo_t[:, 2:3], lo_t[:, 0:1], ALU.subtract)
                e.tensor_tensor(lo_t[:, 4:5], lo_t[:, 4:5], lo_t[:, 3:4], ALU.mult)
                e.tensor_tensor(lo_t[:, 0:1], lo_t[:, 0:1], lo_t[:, 4:5], ALU.add)
                e.tensor_scalar(lo_t[:, 3:4], lo_t[:, 3:4], -1.0, 1.0,
                                ALU.mult, ALU.add)
                e.tensor_tensor(lo_t[:, 4:5], lo_t[:, 2:3], lo_t[:, 1:2], ALU.subtract)
                e.tensor_tensor(lo_t[:, 4:5], lo_t[:, 4:5], lo_t[:, 3:4], ALU.mult)
                e.tensor_tensor(lo_t[:, 1:2], lo_t[:, 1:2], lo_t[:, 4:5], ALU.add)
                e.tensor_tensor(lo_t[:, 2:3], lo_t[:, 0:1], lo_t[:, 1:2], ALU.add)
                return e.tensor_scalar(lo_t[:, 2:3], lo_t[:, 2:3], 0.5, None, ALU.mult)
            uv = s.emit("vector", upd, waits=[mv], inc=("bis", 1))
            last = (it == BISECT_ITERS - 1)
            bv = s.emit("tensor", lambda e, last=last: e.matmul(
                psB[:, 0:1], bdeT[:], lo_t[:, 0:1] if last else lo_t[:, 2:3],
                start=True, stop=True), waits=[uv], inc=("bim", 1))
            s.emit("vector", lambda e: e.tensor_copy(thr[:], psB[:, 0:1]),
                   waits=[bv], inc=("bis", 1))

        bisv = ("bis", s.val("bis"))
        for t in range(NT):
            mv0 = s.emit("vector", lambda e, t=t: e.tensor_scalar(
                mt_f[:], scf[:, 128 * t:128 * (t + 1)], lo_t[:, 0:1], None,
                ALU.is_gt), waits=[bisv, ("mtm", s.val("mtm"))], inc=("mte", 1))
            tv = s.emit("tensor", lambda e, t=t: e.transpose(
                psC[0:128, 0:8], mt_f[:], idn[0:8, 0:8]),
                waits=[mv0, ("mce", s.val("mce")), cs0v], inc=("mtm", 1))
            s.emit("vector", lambda e, t=t: e.tensor_copy(
                mtb[:, t, :], psC[0:128, 0:8]), waits=[tv], inc=("mce", 1))
        mdone = ("mce", s.val("mce"))
        for t in range(NT):
            mv = s.emit("tensor", lambda e, t=t: [e.matmul(
                psD[:, 0:8], utri[:] if tp == t else onet[:], mtb[:, tp, :],
                start=(tp == 0), stop=(tp == t)) for tp in range(t + 1)][-1],
                waits=[mdone, cstv2, ("cse", s.val("cse"))], inc=("csm", 1))
            s.emit("vector", lambda e, t=t: e.tensor_copy(
                ctm[:, t, :], psD[:, 0:8]), waits=[mv], inc=("cse", 1))
        cdone = ("cse", s.val("cse"))
        for t in range(NT):
            def mkidx(e, t=t):
                e.tensor_scalar(stmp[:, 0, :], ctm[:, t, :], 1.0, None, ALU.subtract)
                e.tensor_copy(itmp[:, 0, :], stmp[:, 0, :])
                e.tensor_scalar(itmp[:, 1, :], itmp[:, 0, :], 7, None,
                                ALU.arith_shift_right)
                e.tensor_scalar(itmp[:, 2, :], itmp[:, 1, :], 7, None,
                                ALU.arith_shift_left)
                e.tensor_tensor(itmp[:, 0, :], itmp[:, 0, :], itmp[:, 2, :],
                                ALU.subtract)
                e.tensor_scalar(itmp[:, 2, :], itmp[:, 0, :], 3, None,
                                ALU.arith_shift_left)
                e.tensor_tensor(itmp[:, 2, :], itmp[:, 2, :], itmp[:, 0, :], ALU.add)
                e.tensor_tensor(itmp[:, 2, :], itmp[:, 2, :], itmp[:, 1, :], ALU.add)
                e.tensor_copy(stmp[:, 0, :], itmp[:, 2, :])
                e.tensor_scalar(stmp[:, 0, :], stmp[:, 0, :], -10368.0 - t, None,
                                ALU.add)
                e.tensor_tensor(stmp[:, 1, :], mtb[:, t, :], stmp[:, 0, :], ALU.mult)
                e.tensor_scalar(stmp[:, 1, :], stmp[:, 1, :], 10368.0 + t, None,
                                ALU.add)
                return e.tensor_copy(sidx[:, t, :], stmp[:, 1, :])
            s.emit("vector", mkidx, waits=[cdone], inc=("sie", 1))
        sdone = ("sie", s.val("sie"))
        for e in range(E_MINE):
            dma("gpsimd", idxd_d[e:e + 1].rearrange("o (p c) -> (o p) c", p=128)[0:15, :],
                fill_t[0:15, :], sem="ifl", waits=[zv])
        iflv = ("ifl", s.val("ifl"))
        from concourse.bass_core_rust import IndirectOffsetOnAxis as _IOA
        if tap("thr", (128, 1)) is not None:
            dma("sync", taps["thr"], thr[:], sem="tp1", waits=[bisv])
        s.barrier()
        fpA.pop()
        fpA.pop()
        for t in range(NT):
            for e in range(E_MINE):
                s.emit("gpsimd", lambda eng, t=t, e=e: eng.indirect_dma_start(
                    idxd_d.rearrange("o (a b) -> (o a) b", b=1),
                    _IOA(sidx[:, t, e:e + 1], 0),
                    tokid[:, t:t + 1], None, element_offset=e * 10496),
                    waits=[sdone, iflv, tiv], inc=("isc", 16))
        iscv = ("isc", s.val("isc"))
        if tap("idxd", (E_MINE, 10496), I32) is not None:
            dma("sync", taps["idxd"], idxd_d, sem="tp1", waits=[iscv])

        if "p4" not in phases:
            fin0 = [("xns_d", s.val("xns_d")), ("sct_d", s.val("sct_d"))]
            for nm in ("tp0", "tp1"):
                if s.val(nm):
                    fin0.append((nm, s.val(nm)))
            s.emit("sync", lambda e: _nopi(e), waits=fin0, inc=("done", 1))
            s.finish()
            _finalize_libs(nc)
            return nc, taps
        bfA.push()
        fpA.push()
        pt_sb = bfA.alloc(4, 512)
        o65 = fpA.alloc(512)
        atq = bfA.alloc(512)
        psYb = psY[:, 512:1024]

        pend_ep = []
        n_odd = [0]

        def mk_epilogue(ov, par, kv, slot, qg):
            def doit():
                rv = s.emit("vector", lambda e: e.reciprocal(
                    o65[64:65, :], o65[64:65, :]),
                    waits=[ov], inc=("erc", 1))
                bv = s.emit("tensor", lambda e: e.matmul(
                    psY[0:64, 0:512], ones64[64:65, :],
                    o65[64:65, :], start=True, stop=True),
                    waits=[rv, ("epe", s.val("epe"))], inc=("epm", 1))
                dst = (at_fm[0:64, slot // 2, 512 * qg:512 * (qg + 1)]
                       if par == 0 else atq[0:64, :])
                epv = s.emit("vector", lambda e: e.tensor_tensor(
                    dst, o65[0:64, :], psY[0:64, 0:512], ALU.mult),
                    waits=[bv] + ([("afd", 16 * n_odd[0])] if par else []),
                    inc=("epe", 1))
                if par:
                    dma("sync", at_fm[64:128, slot // 2, 512 * qg:512 * (qg + 1)],
                        atq[0:64, :], sem="afd", waits=[epv])
                    n_odd[0] += 1
            return doit

        NP = NT // 2
        for slot in range(HEADS):
            head = SLOT_TO_HEAD[slot]
            kv = head // 4
            base = 64 * (slot % 2)
            kchunk = slot // 2
            kvchunk = kv // 2
            kvbase = 64 * (kv % 2)
            assert kvbase == base
            par = slot % 2
            for qg in range(4):
                q0 = 512 * qg
                GQ0 = s.val("qkm")

                def emit_pv2(p):
                    bp = (GQ0 + p) % 2
                    s0 = 2 * bp
                    s.emit("tensor", lambda e, p=p, kv=kv, s0=s0: [e.matmul(
                        psC[0:65, :], vtm[:, 2 * p + i, 65 * kv:65 * (kv + 1)],
                        pt_sb[:, s0 + i, :], start=(p == 0 and i == 0),
                        stop=(p == NP - 1 and i == 1)) for i in range(2)][-1],
                        waits=[("pxe", s.val("pxe") - (NP - 1 - p > 0))] +
                              ([("ove", s.val("ove"))] if p == 0 else []),
                        inc=("pvm", 1))
                for p in range(NP):
                    bp = (GQ0 + p) % 2
                    bnk = [(psA, psB), (psD, psYb)][bp]
                    s0 = 2 * bp
                    mv = s.emit("tensor", lambda e, p=p, bnk=bnk, kvchunk=kvchunk,
                                kvbase=kvbase, kchunk=kchunk, base=base, q0=q0: [e.matmul(
                        bnk[i][:],
                        kfm[kvbase:kvbase + 64, kvchunk,
                            128 * (2 * p + i):128 * (2 * p + i + 1)],
                        qfm[base:base + 64, kchunk, q0:q0 + 512],
                        start=True, stop=True) for i in range(2)][-1],
                        waits=[("pxe", s.val("pxe") - 1)], inc=("qkm", 1))
                    s.emit("scalar", lambda e, bnk=bnk, s0=s0: [e.activation(
                        out=pt_sb[:, s0 + i, :], in_=bnk[i][:], func=AF.Exp,
                        bias=nshift[:], scale=0.125) for i in range(2)][-1],
                        waits=[mv, ("pvm", s.val("pvm"))], inc=("pxe", 1),
                        nodrain=True)
                    if p == 6 and pend_ep:
                        pend_ep.pop()()
                    if p > 0:
                        emit_pv2(p - 1)
                emit_pv2(NP - 1)
                ov = s.emit("vector", lambda e: e.tensor_copy(
                    o65[0:65, :], psC[0:65, :]),
                    waits=[("pvm", s.val("pvm")), ("epe", s.val("epe"))],
                    inc=("ove", 1))
                pend_ep.append(mk_epilogue(ov, par, kv, slot, qg))
        pend_ep.pop()()
        s.barrier()
        bfA.pop()
        fpA.pop()
        if tap("at_fm", (128, NDC, HALF), BF16) is not None:
            dma("sync", taps["at_fm"], at_fm, sem="tp2",
                waits=[("bar", s.val("bar"))])

        if "p4b" not in phases:
            fin0 = [("xns_d", s.val("xns_d")), ("sct_d", s.val("sct_d"))]
            for nm in ("tp0", "tp1", "tp2"):
                if s.val(nm):
                    fin0.append((nm, s.val(nm)))
            s.emit("sync", lambda e: _nopi(e), waits=fin0, inc=("done", 1))
            s.finish()
            _finalize_libs(nc)
            return nc, taps
        bfA.push()
        fpA.push()
        wo_sb = bfA.alloc(NDC, D)
        stage = fpA.alloc(2, D)
        bo_rep = fpA.alloc(D)
        bo_rep2 = _reshape(bo_rep, (1, D))

        bov = dma("sync", bo_rep2[0:1, :], bo_d)
        s.emit("gpsimd", lambda e: e.partition_broadcast(bo_rep2, bo_rep2[0:1, :]),
               waits=[bov], inc=("bob", 1))
        wov = dma("gpsimd", wo_sb, wo_d.rearrange("(c p) o -> p c o", p=128),
                  sem="wol", waits=[("afd", 16 * 32)])
        for t in range(NTH):
            tb = t % 2
            pY0 = [psA, psC][tb]
            pY1 = [psB, psD][tb]
            mv = s.emit("tensor", lambda e, t=t, pY0=pY0, pY1=pY1: [e.matmul(
                [pY0, pY1][hh][:], at_fm[:, ic, 128 * t:128 * (t + 1)],
                wo_sb[:, ic, 512 * hh:512 * (hh + 1)],
                start=(ic == 0), stop=(ic == NDC - 1))
                for ic in range(NDC) for hh in range(2)][-1],
                waits=[wov, ("woe", s.val("woe") - 1), ("afd", 16 * 32)],
                inc=("wom", 1))

            def wo_add(e, tb=tb, pY0=pY0, pY1=pY1):
                bo_f = _reshape(bo_rep, (D,))
                e.tensor_tensor(stage[:, tb, 0:512], pY0[:], bo_f[:, 0:512],
                                ALU.add)
                return e.tensor_tensor(stage[:, tb, 512:1024], pY1[:],
                                       bo_f[:, 512:1024], ALU.add)
            wv = s.emit("vector", wo_add,
                        waits=[mv, ("wos", s.val("wos") - 1), ("bob", 1)],
                        inc=("woe", 1))
            dma("scalar", dattn_d[128 * t:128 * (t + 1), :], stage[:, t % 2, :],
                waits=[wv], sem="wo_d" + str(t % 2))
            s.emit("scalar", lambda e: _nopi(e),
                   waits=[("wo_d" + str(t % 2), s.val("wo_d" + str(t % 2)))],
                   inc=("wos", 1))
        s.barrier()
        bfA.pop()
        fpA.pop()
        bfA.pop()

        if "p5" not in phases:
            fin0 = [("xns_d", s.val("xns_d")), ("sct_d", s.val("sct_d")),
                    ("wo_d0", s.val("wo_d0")), ("wo_d1", s.val("wo_d1"))]
            for nm in ("tp0", "tp1", "tp2"):
                if s.val(nm):
                    fin0.append((nm, s.val(nm)))
            s.emit("sync", lambda e: _nopi(e), waits=fin0, inc=("done", 1))
            s.finish()
            _finalize_libs(nc)
            return nc, taps
        bfA.push()
        fpA.push()
        xsel = bfA.alloc(NDC, CAP_PAD)
        h_sb = bfA.alloc(16, CAP_PAD)
        w1_sb = bfA.alloc(NDC, 2048)
        w2_sb = bfA.alloc(16, D)
        y_sb = fpA.alloc(9, D)
        b2_rep = fpA.alloc(D)
        b2_rep2 = _reshape(b2_rep, (1, D))

        from concourse.bass_core_rust import IndirectOffsetOnAxis as _IOA2
        for e in range(E_MINE):
            b2v = dma("sync", b2_rep2[0:1, :], b2_d[e],
                      sem="b2l", waits=[("gfin", s.val("gfin"))])
            s.emit("gpsimd", lambda eng: eng.partition_broadcast(
                b2_rep2, b2_rep2[0:1, :]), waits=[b2v], inc=("b2b", 1))

            ixv = dma("sync", idx_sb[:],
                      idxd_d[e:e + 1, 0:1152].rearrange("o (p c) -> (o p) c", p=128),
                      sem="ixl", waits=[iscv, ("mfr", s.val("mfr"))])
            for c in range(9):
                gb = c % 2
                gx = s.emit("gpsimd", lambda eng, c=c, gb=gb: eng.indirect_dma_start(
                    xg_tm[:, gb, :], None, xn_tm_d, _IOA2(idx_sb[:, c:c + 1], 0)),
                    waits=[ixv, ("xns", NT), ("zrow", 32),
                           ("gte", s.val("gte") - 8)], inc=("mgl" + str(gb), 16))
                gg = s.emit("gpsimd", lambda eng, c=c: eng.indirect_dma_start(
                    gt_t[:, c, :], None, sc_tm_d, _IOA2(idx_sb[:, c:c + 1], 0)),
                    waits=[ixv, ("sts", NT), ("gfin", s.val("gfin"))],
                    inc=("mgg", 16))
                for dc in range(NDC):
                    pst = [psTa, psTb][dc % 2]
                    tv = s.emit("tensor", lambda e2, c=c, dc=dc, gb=gb, p=pst: e2.transpose(
                        p[:, :128], xg_tm[:, gb, 128 * dc:128 * (dc + 1)], idnb[:]),
                        waits=[gx, ("gte", s.val("gte") - 1),
                               ("hsm", s.val("hsm"))], inc=("gtm", 1))
                    s.emit("vector", lambda e2, c=c, dc=dc, p=pst: e2.tensor_copy(
                        xsel[:, dc, 128 * c:128 * (c + 1)], p[:, :128]),
                        waits=[tv], inc=("gte", 1))
            gl = ("gte", s.val("gte"))
            ggl = ("mgg", s.val("mgg"))
            if e == 0 and tap("xsel", (128, NDC, CAP_PAD), BF16) is not None:
                dma("sync", taps["xsel"], xsel, sem="tp3", waits=[gl])
            if e == 0 and tap("gt", (128, 9, 64)) is not None:
                dma("sync", taps["gt"], gt_t[:], sem="tp3", waits=[ggl])

            for hg in range(2):
                w1v = dma("gpsimd", w1_sb,
                          w1_d[e].rearrange("(c p) h -> p c h", p=128)[:, :, 2048 * hg:2048 * (hg + 1)],
                          sem="w1l", waits=[("hsm", s.val("hsm"))])
                w2v = dma("gpsimd", w2_sb,
                          w2_d[e, 2048 * hg:2048 * (hg + 1), :].rearrange("(c p) o -> p c o", p=128),
                          sem="w2l", waits=[("ymm", s.val("ymm"))])
                grp = 0
                for hc in range(16):
                    for (tg, tw) in ((0, 512), (512, 512), (1024, 128)):
                        pp = [psA, psB][grp % 2]
                        mv = s.emit("tensor", lambda e2, hc=hc, tg=tg, tw=tw, pp=pp: [
                            e2.matmul(pp[:, 0:tw], w1_sb[:, dc, 128 * hc:128 * (hc + 1)],
                                      xsel[:, dc, tg:tg + tw],
                                      start=(dc == 0), stop=(dc == NDC - 1))
                            for dc in range(NDC)][-1],
                            waits=[w1v, gl, ("hse", s.val("hse") - 1)],
                            inc=("hsm", 1))
                        s.emit("scalar", lambda e2, hc=hc, tg=tg, tw=tw, pp=pp, hg=hg, e=e: e2.activation(
                            out=h_sb[:, hc, tg:tg + tw], in_=pp[:, 0:tw], func=AF.Gelu,
                            bias=b1_sb[:, e, 16 * hg + hc:16 * hg + hc + 1], scale=1.0),
                            waits=[mv, c0], inc=("hse", 1), nodrain=True)
                        grp += 1
                hv = ("hse", s.val("hse"))
                for ts in range(9):
                    mv = s.emit("tensor", lambda e2, ts=ts: [
                        e2.matmul(psY[:, 512 * hh:512 * (hh + 1)],
                                  h_sb[:, hc, 128 * ts:128 * (ts + 1)],
                                  w2_sb[:, hc, 512 * hh:512 * (hh + 1)],
                                  start=(hc == 0), stop=(hc == 15))
                        for hc in range(16) for hh in range(2)][-1],
                        waits=[w2v, hv, ("yev", s.val("yev"))], inc=("ymm", 1))
                    if hg == 0:
                        s.emit("vector", lambda e2, ts=ts: e2.tensor_copy(
                            y_sb[:, ts, :], psY[:]),
                            waits=[mv, ("msc", s.val("msc"))], inc=("yev", 1))
                    else:
                        s.emit("vector", lambda e2, ts=ts: e2.tensor_tensor(
                            y_sb[:, ts, :], y_sb[:, ts, :], psY[:], ALU.add),
                            waits=[mv], inc=("yev", 1))
            yv = ("yev", s.val("yev"))

            def gate_fin(e2, e=e):
                ins = None
                for ts in range(9):
                    e2.tensor_tensor(y_sb[:, ts, :], y_sb[:, ts, :],
                                     _reshape(b2_rep, (D,)), ALU.add)
                    ins = e2.tensor_scalar(y_sb[:, ts, :], y_sb[:, ts, :],
                                           gt_t[:, ts, e:e + 1], None, ALU.mult)
                return ins
            gv = s.emit("vector", gate_fin, waits=[yv, ("b2b", s.val("b2b")), ggl],
                        inc=("gfin", 1))
            if e == 0 and tap("y0", (128, 9, D)) is not None:
                dma("sync", taps["y0"], y_sb, sem="tp3", waits=[gv])
            mscw = ("msc", s.val("msc"))
            for c in range(9):
                s.emit("gpsimd", lambda eng, e=e, c=c: eng.indirect_dma_start(
                    macc_d, _IOA2(idx_sb[:, c:c + 1], 0), y_sb[:, c, :], None,
                    compute_op=ALU.add),
                    waits=[gv, ("zacc", (NT + 1) * 16), mscw], inc=("msc", 16))
            s.emit("gpsimd", lambda e2: _nopi(e2), waits=[("msc", s.val("msc"))],
                   inc=("mfr", 1))
        bfA.pop()
        fpA.pop()

        RG = [[0, 1], [2, 3], [4, 5], [6, 7]]
        arv = s.emit("gpsimd", lambda e: e.collective_compute(
            "AllReduce", ALU.add, RG, [macc_d[0:S, :]], [maccsum_d]),
            waits=[("msc", s.val("msc")), ("zacc", (NT + 1) * 16)],
            inc=("ccr", 1))
        agv = s.emit("gpsimd", lambda e: e.collective_compute(
            "AllGather", ALU.bypass, RG, [dattn_d], [dattncat_d]),
            waits=[("wo_d0", s.val("wo_d0")), ("wo_d1", s.val("wo_d1"))],
            inc=("ccg", 1))

        dma("sync", tri[:], tri_d, sem="cs0b")
        b1pm = dma("sync", pmask[:], pmask_d, sem="cs0b")
        rmv = dma("sync", rowmap[:], rowmap_d, sem="rml")
        s.emit("vector", lambda e: e.memset(onesb[:], 1.0), inc=("mi", 1))
        s.emit("vector", lambda e: e.memset(S_sb[:], 0.0), inc=("mi", 1))

        def mk_deq(e):
            e.memset(deq_t[:, 0:1], 127.0)
            e.reciprocal(deq_t[:, 0:1], deq_t[:, 0:1])
            return e.tensor_tensor(deq_t[:, 1:2], deq_t[:, 0:1], deq_t[:, 0:1],
                                   ALU.mult)
        s.emit("vector", mk_deq, inc=("mi", 1))
        miv = ("mi", s.val("mi"))

        b_xnfm = bfA.alloc(NDC, S)
        bfA.push()
        b_qfm = bfA.alloc(NDC, HALF)
        b_ktm = bfA.alloc(D)
        b_vtm = bfA.alloc(D)
        b_kfmt = bfA.alloc(NDC, 128)
        b_xnt = bfA.alloc(D)
        b_att = bfA.alloc(2, D)

        fpA.push()
        b_xt = fpA.alloc(4, D)
        b_sqt = fpA.alloc(D)

        for t in range(NT):
            w0 = [("lnD", s.val("lnD")), ("x1s0", s.val("x1s0")),
                  ("x1s1", s.val("x1s1")), rmv, arv, agv]
            l0 = s.emit("gpsimd", lambda e, t=t: e.indirect_dma_start(
                b_xt[:, 0, :], None, xnat_d, _IOA2(rowmap[:, t:t + 1], 0)),
                waits=w0, inc=("bxl0", 16))
            l1 = s.emit("gpsimd", lambda e, t=t: e.indirect_dma_start(
                b_xt[:, 1, :], None, dattncat_d, _IOA2(rowmap[:, t:t + 1], 0)),
                waits=w0, inc=("bxl1", 16))
            l2 = s.emit("gpsimd", lambda e, t=t: e.indirect_dma_start(
                b_xt[:, 2, :], None, maccsum_d, _IOA2(rowmap[:, t:t + 1], 0)),
                waits=w0, inc=("bxl2", 16))

            def assemble(e):
                e.tensor_tensor(b_xt[:, 0, :], b_xt[:, 0, :], b_xt[:, 1, :], ALU.add)
                e.tensor_tensor(b_xt[:, 0, :], b_xt[:, 0, :], b_xt[:, 2, :], ALU.add)
                return e.reduce_sum(ln_m[:, 0:1], b_xt[:, 0, :], axis=AX.X)
            asv = s.emit("vector", assemble, waits=[l0, l1, l2], inc=("asm", 1))
            if t >= NTH:
                dma("scalar", x1_d[128 * (t - NTH):128 * (t - NTH + 1), :],
                    b_xt[:, 0, :], sem="x1s" + str(t % 2), waits=[asv])
            sv = s.emit("scalar", lambda e: e.activation(
                out=b_sqt, in_=b_xt[:, 0, :], func=AF.Square, accum_out=ln_m[:, 1:2]),
                waits=[asv, ("lnA", s.val("lnA"))], inc=("lnB", 1))

            def ln_stats(e):
                e.tensor_scalar(ln_m[:, 2:3], ln_m[:, 0:1], 1.0 / D, None, ALU.mult)
                e.tensor_scalar(ln_m[:, 3:4], ln_m[:, 1:2], 1.0 / D, None, ALU.mult)
                e.tensor_tensor(ln_m[:, 5:6], ln_m[:, 2:3], ln_m[:, 2:3], ALU.mult)
                e.tensor_tensor(ln_m[:, 6:7], ln_m[:, 3:4], ln_m[:, 5:6], ALU.subtract)
                return e.tensor_scalar(ln_m[:, 6:7], ln_m[:, 6:7], 1e-5, None, ALU.add)
            av = s.emit("vector", ln_stats, waits=[("lnB", s.val("lnB"))], inc=("lnA", 1))
            sv2 = s.emit("scalar", lambda e: e.activation(
                out=ln_m[:, 7:8], in_=ln_m[:, 6:7], func=AF.Sqrt),
                waits=[av, ("lnD", s.val("lnD"))], inc=("lnC", 1))

            def ln_norm(e):
                e.reciprocal(ln_m[:, 7:8], ln_m[:, 7:8])
                return e.tensor_scalar(b_xnt, b_xt[:, 0, :], ln_m[:, 2:3], ln_m[:, 7:8],
                                       ALU.subtract, ALU.mult)
            lnv = s.emit("vector", ln_norm, waits=[sv2, ("tpe", s.val("tpe"))],
                         inc=("lnD", 1))
            for dc in range(NDC):
                pst = [psTa, psTb][dc % 2]
                tv = s.emit("tensor", lambda e, dc=dc, p=pst: e.transpose(
                    p[:, :128], b_xnt[:, 128 * dc:128 * (dc + 1)], idnb[:]),
                    waits=[lnv, ("tpe", s.val("tpe") - 1), cstv], inc=("tpm", 1))
                s.emit("vector", lambda e, dc=dc, t=t, p=pst: e.tensor_copy(
                    b_xnfm[:, dc, 128 * t:128 * (t + 1)], p[:, :128]),
                    waits=[tv], inc=("tpe", 1))
        s.barrier()
        fpA.pop()

        fpA.push()
        el_t = fpA.alloc(2, D)
        bfA.push()
        b_wq = bfA.alloc(NDC, D)

        wqv = dma("gpsimd", b_wq, wq1_d.rearrange("(c p) o -> p c o", p=128), sem="wql")

        for oc in range(NDC):
            for g in range(4):
                mv = s.emit("tensor", lambda e, oc=oc, g=g: [e.matmul(
                    psD[:], b_wq[:, dc, 128 * oc:128 * (oc + 1)],
                    b_xnfm[:, dc, HALF + 512 * g:HALF + 512 * (g + 1)],
                    start=(dc == 0), stop=(dc == NDC - 1)) for dc in range(NDC)][-1],
                    waits=[wqv, ("qpe", s.val("qpe"))], inc=("qpm", 1))

                def q_elu(e, oc=oc, g=g):
                    lo = 512 * g
                    e.tensor_scalar(el_t[:, 0, 0:512], psD[:], 0.0, None, ALU.min)
                    return e.tensor_copy(b_qfm[:, oc, lo:lo + 512], psD[:])
                ev1 = s.emit("vector", q_elu,
                             waits=[mv, ("qel", s.val("qel")),
                                    ("x1s0", NTH // 2 * 16), ("x1s1", NTH // 2 * 16)],
                             inc=("qpe", 1))
                ev2 = s.emit("scalar", lambda e: e.activation(
                    out=el_t[:, 1, 0:512], in_=el_t[:, 0, 0:512], func=AF.Exp),
                    waits=[ev1], inc=("qex", 1), nodrain=True)
                s.emit("vector", lambda e, oc=oc, g=g: e.scalar_tensor_tensor(
                    b_qfm[:, oc, 512 * g:512 * (g + 1)], b_qfm[:, oc, 512 * g:512 * (g + 1)],
                    0.0, el_t[:, 1, 0:512], ALU.max, ALU.add),
                    waits=[ev2], inc=("qel", 1))
        s.barrier()
        bfA.pop()

        bfA.push()
        wk_sb = bfA.alloc(NDC, D)
        wv_sb = bfA.alloc(NDC, D)
        r8 = fpA.alloc(16)
        wkv_ = dma("gpsimd", wk_sb, wk1_d.rearrange("(c p) o -> p c o", p=128), sem="wkl")
        wvv = dma("gpsimd", wv_sb, wv1_d.rearrange("(c p) o -> p c o", p=128), sem="wvl")
        for c in range(NT):
            kmv = s.emit("tensor", lambda e, c=c: [e.matmul(
                psY[:, 512 * hh:512 * (hh + 1)], b_xnfm[:, dc, 128 * c:128 * (c + 1)],
                wk_sb[:, dc, 512 * hh:512 * (hh + 1)],
                start=(dc == 0), stop=(dc == NDC - 1))
                for dc in range(NDC) for hh in range(2)][-1],
                waits=[wkv_, ("kue", s.val("kue"))], inc=("kpm", 1))

            def k_elu_a(e, c=c):
                e.tensor_scalar(el_t[:, 0, :], psY[:], 0.0, None, ALU.min)
                return e.tensor_copy(b_ktm, psY[:])
            k1 = s.emit("vector", k_elu_a,
                        waits=[kmv, ("fte", s.val("fte")), ("sde", s.val("sde"))],
                        inc=("kue", 1))
            k2 = s.emit("scalar", lambda e: e.activation(
                out=el_t[:, 1, :], in_=el_t[:, 0, :], func=AF.Exp),
                waits=[k1], inc=("kex", 1), nodrain=True)

            def k_elu_b(e, c=c):
                e.scalar_tensor_tensor(b_ktm, b_ktm, 0.0, el_t[:, 1, :], ALU.max, ALU.add)
                return e.tensor_scalar(b_ktm, b_ktm, pmask[:, c:c + 1], None, ALU.mult)
            k3 = s.emit("vector", k_elu_b, waits=[k2, b1pm], inc=("kue", 1))
            vmv = s.emit("tensor", lambda e, c=c: [e.matmul(
                [psA, psB][hh][:], b_xnfm[:, dc, 128 * c:128 * (c + 1)],
                wv_sb[:, dc, 512 * hh:512 * (hh + 1)],
                start=(dc == 0), stop=(dc == NDC - 1))
                for dc in range(NDC) for hh in range(2)][-1],
                waits=[wvv, ("vpe", s.val("vpe")), ("sma", s.val("sma")),
                       ("eda", s.val("eda")), ("ssu", s.val("ssu"))],
                inc=("vpm", 1))

            def v_evac(e):
                e.tensor_copy(b_vtm[:, 0:512], psA[:])
                return e.tensor_copy(b_vtm[:, 512:1024], psB[:])
            v1 = s.emit("vector", v_evac,
                        waits=[vmv, ("spv", s.val("spv")), ("sde", s.val("sde"))],
                        inc=("vpe", 1))
            if c >= NTH:
                for dc in range(NDC):
                    pst = [psTa, psTb][dc % 2]
                    tv = s.emit("tensor", lambda e, dc=dc, p=pst: e.transpose(
                        p[:, :128], b_ktm[:, 128 * dc:128 * (dc + 1)], idnb[:]),
                        waits=[k3, cstv, ("fte", s.val("fte") - 1),
                               ("sam", s.val("sam"))], inc=("ftm", 1))
                    s.emit("vector", lambda e, dc=dc, p=pst: e.tensor_copy(
                        b_kfmt[:, dc, :], p[:, :128]), waits=[tv], inc=("fte", 1))
                kfv = ("fte", s.val("fte"))
                cq = 128 * (c - NTH)

                sbv = s.emit("vector", lambda e: [e.tensor_copy(
                    S_bf[:, f2, :], S_sb[:, f2, :]) for f2 in range(8)][-1],
                    waits=[("ssu", s.val("ssu"))], inc=("sbf", 1))

                for hg2 in range(2):
                    hs = 8 * hg2

                    def inter_f(e, hs, cq, par2):
                        ins = None
                        for lh in range(par2, 8, 2):
                            h = hs + lh
                            b2 = 64 * (h % 2)
                            f2 = h // 2
                            ins = e.matmul(psC[0:128, 64 * lh:64 * lh + 64],
                                           b_qfm[b2:b2 + 64, f2, cq:cq + 128],
                                           S_bf[b2:b2 + 64, f2, 0:64],
                                           start=(lh == 0), stop=False)
                            ins = e.matmul(psB[0:128, h:h + 1],
                                           b_qfm[b2:b2 + 64, f2, cq:cq + 128],
                                           S_bf[b2:b2 + 64, f2, 64:65],
                                           start=(lh == 0), stop=False)
                        return ins
                    iv0 = s.emit("tensor",
                                 lambda e, hs=hs, cq=cq: inter_f(e, hs, cq, 0),
                                 waits=[sbv, v1, ("epv", s.val("epv")),
                                        ("eda", s.val("eda")),
                                        ("bar", s.val("bar"))], inc=("sim_", 1))
                    iv = s.emit("tensor",
                                lambda e, hs=hs, cq=cq: inter_f(e, hs, cq, 1),
                                waits=[iv0], inc=("sim_", 1))

                    mstep = []
                    for par2 in range(2):
                        pbank = [psD, psA][par2]

                        def at_f(e, hs=hs, par2=par2, pbank=pbank, cq=cq):
                            ins = None
                            for j in range(4):
                                lh = 2 * j + par2
                                h = hs + lh
                                b2 = 64 * (h % 2)
                                f2 = h // 2
                                ins = e.matmul(pbank[:, 128 * j:128 * (j + 1)],
                                               b_kfmt[b2:b2 + 64, f2, :],
                                               b_qfm[b2:b2 + 64, f2, cq:cq + 128],
                                               start=True, stop=True)
                            return ins
                        wts = [kfv, ("sma", s.val("sma"))]
                        if par2 == 1:
                            wts.append(v1)
                        av_ = s.emit("tensor", at_f, waits=wts, inc=("sam", 1))
                        mv_ = s.emit("vector", lambda e, par2=par2, pbank=pbank: [
                            e.tensor_tensor(
                                b_xnt[:, 128 * (2 * j + par2):128 * (2 * j + par2 + 1)],
                                pbank[:, 128 * j:128 * (j + 1)], tri[:], ALU.mult)
                            for j in range(4)][-1],
                            waits=[av_, b1pm, ("spv", s.val("spv"))],
                            inc=("sma", 1), nodrain=True)
                        mstep.append(mv_)

                    def intra_f(e, hs=hs):
                        ins = None
                        for lh in range(8):
                            h = hs + lh
                            xs = b_xnt[:, 128 * lh:128 * (lh + 1)]
                            ins = e.matmul(psC[0:128, 64 * lh:64 * lh + 64], xs,
                                           b_vtm[:, 64 * h:64 * (h + 1)],
                                           start=False, stop=(lh == 7))
                            ins = e.matmul(psB[0:128, h:h + 1], xs, onesb[:],
                                           start=False, stop=(lh == 7))
                        return ins
                    i5 = s.emit("tensor", intra_f, waits=[mstep[0], mstep[1], iv],
                                inc=("spv", 1))

                    e1 = s.emit("vector", lambda e, hs=hs: e.tensor_scalar(
                        r8[:, hs:hs + 8], psB[0:128, hs:hs + 8], 1e-6, None,
                        ALU.add), waits=[i5], inc=("eda", 1))
                    e2 = s.emit("vector", lambda e, hs=hs: e.reciprocal(
                        r8[:, hs:hs + 8], r8[:, hs:hs + 8]),
                        waits=[e1], inc=("edb", 1))
                    s.emit("vector", lambda e, hs=hs, c=c: [e.tensor_scalar(
                        b_att[:, c % 2, 64 * (hs + lh):64 * (hs + lh + 1)],
                        psC[0:128, 64 * lh:64 * (lh + 1)], r8[:, hs + lh:hs + lh + 1],
                        None, ALU.mult) for lh in range(8)][-1],
                        waits=[e2, ("ats" + str(c % 2), s.val("ats" + str(c % 2)))],
                        inc=("epv", 1))

            for hg2 in range(2):
                hs = 8 * hg2

                def upd_f(e, hs=hs):
                    ins = None
                    for lh in range(8):
                        h = hs + lh
                        b2 = 64 * (h % 2)
                        col = 16 + 65 * (lh // 2)
                        ins = e.matmul(psB[b2:b2 + 64, col:col + 64],
                                       b_ktm[:, 64 * h:64 * (h + 1)],
                                       b_vtm[:, 64 * h:64 * (h + 1)],
                                       start=True, stop=False)
                        ins = e.matmul(psB[b2:b2 + 64, col + 64:col + 65],
                                       b_ktm[:, 64 * h:64 * (h + 1)], onesb[:],
                                       start=False, stop=True)
                    return ins
                u1 = s.emit("tensor", upd_f,
                            waits=[k3, v1, ("ssu", s.val("ssu")),
                                   ("eda", s.val("eda"))], inc=("sde", 1))
                s.emit("vector", lambda e, hs=hs: [e.tensor_tensor(
                    S_sb[:, 4 * hg3 + f2, :], S_sb[:, 4 * hg3 + f2, :],
                    psB[:, 16 + 65 * f2:16 + 65 * f2 + 65], ALU.add)
                    for hg3 in [hs // 8] for f2 in range(4)][-1],
                    waits=[u1], inc=("ssu", 1))
            if c >= NTH:
                dma("scalar", attn1_d[128 * (c - NTH):128 * (c - NTH + 1), :],
                    b_att[:, c % 2, :], sem="ats" + str(c % 2),
                    waits=[("epv", s.val("epv"))])
        s.barrier()
        bfA.pop()
        fpA.pop()
        bfA.pop()

        bfA.push()
        fpA.push()
        b_atfm = bfA.alloc(2, NDC, 128)
        at_ld = bfA.alloc(2, D)
        b_wo = bfA.alloc(NDC, D)
        x1t = fpA.alloc(2, D)
        wov = dma("gpsimd", b_wo, wo1_d.rearrange("(c p) o -> p c o", p=128), sem="bwol")
        for t in range(NTH):
            tb = t % 2
            atv = dma("sync", at_ld[:, tb, :], attn1_d[128 * t:128 * (t + 1), :],
                      sem="atl" + str(tb),
                      waits=[("fte", s.val("fte") - 8),
                             ("ats0", NTH // 2 * 16), ("ats1", NTH // 2 * 16)])
            x1v = dma("scalar", x1t[:, tb, :], x1_d[128 * t:128 * (t + 1), :],
                      sem="x1l" + str(tb),
                      waits=[("woe", s.val("woe") - 1),
                             ("os" + str(tb), s.val("os" + str(tb))),
                             ("x1s0", NTH // 2 * 16), ("x1s1", NTH // 2 * 16)])
            for dc in range(NDC):
                pst = [psTa, psTb][dc % 2]
                tv = s.emit("tensor", lambda e, tb=tb, dc=dc, p=pst: e.transpose(
                    p[:, :128], at_ld[:, tb, 128 * dc:128 * (dc + 1)], idnb[:]),
                    waits=[atv, ("fte", s.val("fte") - 1),
                           ("wom", s.val("wom") - 1)],
                    inc=("ftm", 1))
                s.emit("vector", lambda e, tb=tb, dc=dc, p=pst: e.tensor_copy(
                    b_atfm[:, tb, dc, :], p[:, :128]), waits=[tv], inc=("fte", 1))
            mv = s.emit("tensor", lambda e, tb=tb: [e.matmul(
                psY[:, 512 * hh:512 * (hh + 1)], b_atfm[:, tb, ic, :],
                b_wo[:, ic, 512 * hh:512 * (hh + 1)],
                start=(ic == 0), stop=(ic == NDC - 1))
                for ic in range(NDC) for hh in range(2)][-1],
                waits=[("fte", s.val("fte")), wov, ("woe", s.val("woe"))],
                inc=("wom", 1))
            wv = s.emit("vector", lambda e, tb=tb: e.tensor_tensor(
                x1t[:, tb, :], x1t[:, tb, :], psY[:], ALU.add),
                waits=[mv, x1v], inc=("woe", 1))
            dma("scalar", out_d[128 * t:128 * (t + 1), :], x1t[:, tb, :],
                waits=[wv], sem="os" + str(tb))
        s.barrier()
        bfA.pop()
        fpA.pop()

        bfA.push()
        fpA.push()
        wgt = bfA.alloc(2, NDC, 128)
        wut = bfA.alloc(2, NDC, 128)
        wd_sb = bfA.alloc(12, D)
        hact = bfA.alloc(12, 2048)
        sg_t = bfA.alloc(512)
        yst = fpA.alloc(2, D)

        gdone = {}
        udone = {}
        it = 0
        dn_h0 = 0
        for hh2 in range(2):
            for hc in range(12):
                hcg = 12 * hh2 + hc
                hb = hcg % 2
                g1 = dma("gpsimd", wgt[:, hb],
                         wg_d.rearrange("(c p) o -> p c o", p=128)[:, :, 128 * hcg:128 * (hcg + 1)],
                         sem="wgl" + str(hb),
                         waits=([gdone[hcg - 2]] if hcg - 2 in gdone else []))
                g2 = dma("gpsimd", wut[:, hb],
                         wu_d.rearrange("(c p) o -> p c o", p=128)[:, :, 128 * hcg:128 * (hcg + 1)],
                         sem="wul" + str(hb),
                         waits=([udone[hcg - 2]] if hcg - 2 in udone else []))
                dma("gpsimd", wd_sb[:, hc, :], wd_d[128 * hcg:128 * (hcg + 1), :],
                    sem="wdl0",
                    waits=([("dnm", dn_h0)] if hh2 == 1 else []))
                for tg in range(4):
                    pg = [psA, psB][it % 2]
                    pu = [psC, psD][it % 2]
                    tg0 = HALF + 512 * tg
                    mv = s.emit("tensor", lambda e, hb=hb, tg0=tg0, pg=pg: [e.matmul(
                        pg[:], wgt[:, hb, dc, :], b_xnfm[:, dc, tg0:tg0 + 512],
                        start=(dc == 0), stop=(dc == NDC - 1)) for dc in range(NDC)][-1],
                        waits=[g1, ("sge", s.val("sge") - 1)], inc=("gm_", 1))
                    mv2 = s.emit("tensor", lambda e, hb=hb, tg0=tg0, pu=pu: [e.matmul(
                        pu[:], wut[:, hb, dc, :], b_xnfm[:, dc, tg0:tg0 + 512],
                        start=(dc == 0), stop=(dc == NDC - 1)) for dc in range(NDC)][-1],
                        waits=[g2, ("hae", s.val("hae") - 1)], inc=("gum", 1))
                    sv = s.emit("scalar", lambda e, pg=pg: e.activation(
                        out=sg_t, in_=pg[:], func=AF.Silu, scale=deq_t[:, 0:1]),
                        waits=[mv, ("hae", s.val("hae")), miv], inc=("sge", 1),
                        nodrain=True)
                    wmul = [sv, mv2]
                    if hh2 == 1:
                        wmul.append(("dnm", dn_h0))
                    s.emit("vector", lambda e, hc=hc, tg=tg, pu=pu: e.tensor_tensor(
                        hact[:, hc, 512 * tg:512 * (tg + 1)], sg_t, pu[:], ALU.mult),
                        waits=wmul, inc=("hae", 1))
                    it += 1
                gdone[hcg] = mv
                udone[hcg] = mv2
            hdone = ("hae", s.val("hae"))
            for tc in range(16):
                tb = tc % 2
                dn = s.emit("tensor", lambda e, tc=tc: [
                    e.matmul(psY[:, 512 * hh:512 * (hh + 1)],
                             hact[:, hc, 128 * tc:128 * (tc + 1)],
                             wd_sb[:, hc, 512 * hh:512 * (hh + 1)],
                             start=(hc == 0), stop=(hc == 11))
                    for hc in range(12) for hh in range(2)][-1],
                    waits=[("wdl0", 16 * 12 * (hh2 + 1)), hdone,
                           ("dne", s.val("dne"))], inc=("dnm", 1))
                yv = s.emit("vector", lambda e, tb=tb: e.tensor_scalar(
                    yst[:, tb, :], psY[:], deq_t[:, 1:2], None, ALU.mult),
                    waits=[dn, ("ya" + str(tb), s.val("ya" + str(tb))),
                           ("os0", NTH // 2 * 16), ("os1", NTH // 2 * 16)],
                    inc=("dne", 1))
                av = s.emit("gpsimd", lambda e, tc=tc, tb=tb: e.dma_start(
                    out=out_d[128 * tc:128 * (tc + 1), :], in_=yst[:, tb, :],
                    accum_op=ALU.add),
                    waits=[yv, ("os0", NTH // 2 * 16), ("os1", NTH // 2 * 16)],
                    inc=("ya" + str(tb), 16))
            dn_h0 = s.val("dnm")
        bfA.pop()
        fpA.pop()

        fin = [("ya0", s.val("ya0")), ("ya1", s.val("ya1")),
               ("os0", s.val("os0")), ("os1", s.val("os1"))]
        for nm in ("tp0", "tp1", "tp2", "tp3"):
            if s.val(nm):
                fin.append((nm, s.val(nm)))
        s.emit("sync", lambda e: _nopi(e), waits=fin, inc=("done", 1))


    s.finish()
    _finalize_libs(nc)
    import os as _os
    if _os.environ.get("KDBG_SEMS"):
        print("SEMS:", len(s.handles), sorted(s.handles))
    return nc, taps



_CACHE = {}


def _prep_fused_inputs(inputs):
    cos2, sin2, coslo, sinlo, rotT, idn, bde, bdeT = host_consts()
    x = np.ascontiguousarray(np.asarray(inputs["L0_attn_x"], dtype=np.float32))
    wq = np.asarray(inputs["L0_attn_Wq"], dtype=np.float32)
    wkv = np.ascontiguousarray(np.asarray(inputs["L0_attn_Wkv"], dtype=np.float32))
    wo = np.asarray(inputs["L0_attn_Wo"], dtype=np.float32)
    bo = np.ascontiguousarray(np.asarray(inputs["L0_attn_bo"], dtype=np.float32).reshape(1, D))
    rw = np.asarray(inputs["L0_ffn_router_w"], dtype=np.float32)
    w1 = np.asarray(inputs["L0_ffn_w1"], dtype=np.float32)
    b1 = np.asarray(inputs["L0_ffn_b1"], dtype=np.float32)
    w2 = np.asarray(inputs["L0_ffn_w2"], dtype=np.float32)
    b2 = np.asarray(inputs["L0_ffn_b2"], dtype=np.float32)

    wq_p = np.empty_like(wq)
    wo_p = np.empty_like(wo)
    for h in range(HEADS):
        sl = SIGMA[h]
        wq_p[:, 64 * sl:64 * (sl + 1)] = wq[:, 64 * h:64 * (h + 1)]
        wo_p[64 * sl:64 * (sl + 1), :] = wo[64 * h:64 * (h + 1), :]
    wq_p = np.ascontiguousarray(wq_p)
    wo_p = np.ascontiguousarray(wo_p)
    utri = np.ascontiguousarray(np.triu(np.ones((128, 128), np.float32)))
    onef = np.ones((128, 128), np.float32)
    tri = np.ascontiguousarray(np.triu(np.ones((128, 128), np.float32)))
    wg = np.ascontiguousarray(np.asarray(inputs["L1_ffn_w_gate"], np.int32).T)
    wu = np.ascontiguousarray(np.asarray(inputs["L1_ffn_w_up"], np.int32).T)
    wd = np.ascontiguousarray(np.asarray(inputs["L1_ffn_w_down"], np.int32).T)

    in_maps = []
    for c in range(8):
        b = c // 2
        hf = c % 2
        ordr = (np.concatenate([np.arange(HALF, S), np.arange(0, HALF)])
                if hf == 1 else np.arange(S))
        myex = [0, 1, 2, 3] if hf == 0 else [4, 5, 6, 7]
        exord = myex + [e for e in range(E_TOT) if e not in myex]
        tokid = ((np.arange(S) + hf * HALF) % S).astype(np.int32)
        tokid = np.ascontiguousarray(tokid.reshape(NT, 128).T)
        if hf == 0:
            rmap = np.concatenate([np.zeros(HALF, np.int64), np.arange(0, HALF)])
            pm = np.concatenate([np.zeros((128, NTH), np.float32),
                                 np.ones((128, NTH), np.float32)], 1)
        else:
            rmap = np.arange(S)
            pm = np.ones((128, NT), np.float32)
        rmap = np.ascontiguousarray(rmap.astype(np.int32).reshape(NT, 128).T)
        m = {
            "x": np.ascontiguousarray(x[b][ordr]),
            "wq": wq_p, "wkv": wkv, "wo": wo_p, "bo": bo,
            "rwt": np.ascontiguousarray(rw[exord].T),
            "w1": np.ascontiguousarray(w1[myex]),
            "b1": np.ascontiguousarray(
                b1[myex].reshape(E_MINE, MOE_HID // 128, 128).transpose(0, 2, 1)),
            "w2": np.ascontiguousarray(w2[myex]),
            "b2": np.ascontiguousarray(b2[myex].reshape(E_MINE, 1, D)),
            "cos": np.ascontiguousarray(cos2[:, ordr]),
            "sin": np.ascontiguousarray(sin2[:, ordr]),
            "coslo": np.ascontiguousarray(coslo[:, ordr]),
            "sinlo": np.ascontiguousarray(sinlo[:, ordr]),
            "rot": rotT, "idn": idn, "bde": bde, "bdeT": bdeT,
            "utri": utri, "onef": onef,
            "tokid_in": tokid,
            "xnat": x[b],
            "rowmap": rmap,
            "wq1": np.asarray(inputs["L1_attn_wq"], np.float32),
            "wk1": np.asarray(inputs["L1_attn_wk"], np.float32),
            "wv1": np.asarray(inputs["L1_attn_wv"], np.float32),
            "wo1": np.asarray(inputs["L1_attn_wo"], np.float32),
            "wg": wg, "wu": wu, "wd": wd,
            "tri": tri, "pmask": pm,
        }
        in_maps.append(m)
    return in_maps


def kernel(**inputs):
    key = "fused"
    if key not in _CACHE:
        _CACHE[key] = build_fused()
    nc, taps = _CACHE[key]
    in_maps = _prep_fused_inputs(inputs)
    res = run_bass_kernel_spmd(nc, in_maps, core_ids=list(range(8)))
    out = np.empty((B, S, D), np.float32)
    for b in range(B):
        out[b, :HALF] = res.results[2 * b]["out"]
        out[b, HALF:] = res.results[2 * b + 1]["out"]
    return out


# version 19
# speedup vs baseline: 1.5147x; 1.0092x over previous
import contextlib
import numpy as np
import ml_dtypes
import concourse.bass as bass
import concourse.mybir as mybir
from concourse.bass_utils import run_bass_kernel_spmd
from concourse.expressions import smin
import bass_rust as _bass_rust
from concourse.library_config import all_libraries, standard as _std_lib


def _finalize_libs(nc):
    inst_map = {}
    for lib in all_libraries:
        for it in lib.instructions:
            inst_map[it] = inst_map.get(it, 0) | (1 << lib.index)
    _bass_rust.insert_library_loads(nc, inst_map, len(all_libraries), _std_lib.index)
    mybir.codegen_inst_isa_subclasses(nc)

F32 = mybir.dt.float32
BF16 = mybir.dt.bfloat16
I16 = mybir.dt.int16
I32 = mybir.dt.int32
AF = mybir.ActivationFunctionType
ALU = mybir.AluOpType
AX = mybir.AxisListType

B, S, D = 4, 4096, 1024
HALF = S // 2
NT = S // 128
NTH = HALF // 128
NDC = D // 128
HEADS, KV_HEADS, HD = 16, 4, 64
E_TOT, E_MINE = 8, 4
MOE_HID = 4096
CAP = int(S * 2.0 / E_TOT) + 1
CAP_PAD = 1152
L1_HID = 3072
DEQ = float(np.float32(1.0 / 127.0))
EXP_SHIFT = 15.0
BISECT_ITERS = 40

SIGMA = [0] * HEADS
_ev = iter([0, 2, 4, 6, 8, 10, 12, 14])
_od = iter([1, 3, 5, 7, 9, 11, 13, 15])
for _h in range(HEADS):
    SIGMA[_h] = next(_ev) if (_h // 4) % 2 == 0 else next(_od)
SLOT_TO_HEAD = [0] * HEADS
for _h2, _s2 in enumerate(SIGMA):
    SLOT_TO_HEAD[_s2] = _h2

ENGINES = ("sync", "scalar", "vector", "tensor", "gpsimd")

_DRAIN_OPS = frozenset({
    "tensor_tensor", "tensor_add", "tensor_sub", "tensor_mul", "tensor_max",
    "tensor_scalar", "tensor_single_scalar", "tensor_scalar_add",
    "tensor_scalar_sub", "tensor_scalar_mul", "tensor_scalar_max",
    "tensor_scalar_min", "tensor_copy", "tensor_reduce", "reduce_max",
    "reduce_sum", "scalar_tensor_tensor", "tensor_tensor_scan", "memset",
    "memzero", "reciprocal", "select", "copy_predicated", "activation",
    "copy", "add", "mul", "square", "sqrt", "sign", "iota",
    "partition_broadcast", "partition_all_reduce", "sparse_gather",
    "kth_largest", "dma_gather", "dma_scatter_add", "value_load",
    "transpose", "stream_shuffle",
})


class _DrainingEngine:

    def __init__(self, eng):
        object.__setattr__(self, "_e", eng)
        object.__setattr__(self, "_dirty", False)
        object.__setattr__(self, "_skip", False)

    def __getattr__(self, name):
        attr = getattr(self._e, name)
        if name in _DRAIN_OPS and callable(attr):
            def wrapped(*a, **kw):
                if (object.__getattribute__(self, "_dirty")
                        and not object.__getattribute__(self, "_skip")):
                    self._e.drain(fusable=True)
                object.__setattr__(self, "_dirty", True)
                return attr(*a, **kw)
            return wrapped
        return attr


class Sched:

    def __init__(self, nc, ctx):
        self.nc = nc
        self.items = []
        self.counts = {}
        self.handles = {}
        self.alias = {}
        self._ctx = ctx

    DEDICATED = frozenset({
        "afd",
        "bob",
        "atl0",
        "atl1",
        "ats0",
        "ats1",
        "b1l",
        "b2l",
        "bar",
        "bwol",
        "bxl0",
        "bxl1",
        "bxl2",
        "cs0",
        "cs0b",
        "csl",
        "cst",
        "done",
        "ifl",
        "isc",
        "ixl",
        "ld_gpsimd",
        "ld_scalar",
        "ld_sync",
        "mgg",
        "mgl0",
        "mgl1",
        "msc",
        "os0",
        "os1",
        "rml",
        "rwl",
        "scg",
        "sct_d",
        "sfd",
        "sts",
        "tkl",
        "tp0",
        "tp1",
        "tp2",
        "tp3",
        "w1l",
        "w2l",
        "wc",
        "wdl0",
        "wdl1",
        "wgl0",
        "wgl1",
        "wkl",
        "wo_d0",
        "wo_d1",
        "wol",
        "wql",
        "wul0",
        "wul1",
        "wvl",
        "x1l0",
        "x1l1",
        "x1s0",
        "x1s1",
        "xl0",
        "xl1",
        "xl2",
        "xns",
        "xns_d",
        "ya0",
        "ya1",
        "zacc",
        "zrow",
    })
    POOL = 30

    def _phys(self, name):
        if name in self.DEDICATED or (name.startswith("sp") and name[2:].isdigit()):
            return name
        if name not in self.alias:
            self.alias[name] = "sp%d" % (len(self.alias) % self.POOL)
        return self.alias[name]

    def sem(self, name):
        name = self._phys(name)
        if name not in self.handles:
            self.handles[name] = self._ctx.enter_context(self.nc.semaphore(name))
            self.counts[name] = 0
        return self.handles[name]

    def val(self, name):
        name = self._phys(name)
        self.sem(name)
        return self.counts[name]

    def emit(self, eng, fn, waits=(), inc=None, nodrain=False):
        ws = []
        for w in waits:
            if w is None:
                continue
            (nm, v) = w
            nm = self._phys(nm)
            self.sem(nm)
            if v > 0:
                ws.append((nm, v))
        incs = []
        if inc is not None:
            nm, n = inc
            nm = self._phys(nm)
            self.sem(nm)
            self.counts[nm] += n
            incs.append((nm, n))
        self.items.append((eng, fn, ws, incs, nodrain))
        return (incs[0][0], self.counts[incs[0][0]]) if inc else None

    def barrier(self):
        nm = "bar"
        self.sem(nm)
        target = self.counts[nm] + len(ENGINES)
        for e in ENGINES:
            self.counts[nm] += 1
            self.items.append((e, "nop", [], [(nm, 1)], False))
        for e in ENGINES:
            self.items.append((e, "wait", [(nm, target)], [], False))

    def dma(self, q, out, in_, waits=(), sem=None):
        nm = sem if sem is not None else ("ld_" + q)
        return self.emit(q, lambda e, o=out, i=in_: e.dma_start(out=o, in_=i),
                         waits=waits, inc=(nm, 16))

    def run(self, engname, eng):
        if engname in ("vector", "scalar", "gpsimd"):
            eng = _DrainingEngine(eng)
        for (e, fn, waits, incs, nodrain) in self.items:
            if e != engname:
                continue
            if isinstance(eng, _DrainingEngine):
                object.__setattr__(eng, "_skip", nodrain)
            for (nm, v) in waits:
                eng.wait_ge(self.handles[nm], v)
            if fn == "nop":
                ins = eng.engine_nop() if hasattr(eng, "engine_nop") else eng.nop()
            elif fn == "wait":
                continue
            else:
                ins = fn(eng)
            if incs:
                if isinstance(ins, (list, tuple)):
                    ins = ins[-1]
                assert ins is not None
                for (nm, n) in incs:
                    ins.then_inc(self.handles[nm], n)

    def finish(self):
        nc = self.nc
        with nc.Block() as block:
            @block.sync
            def _(e):
                self.run("sync", e)

            @block.scalar
            def _(e):
                self.run("scalar", e)

            @block.vector
            def _(e):
                self.run("vector", e)

            @block.tensor
            def _(e):
                self.run("tensor", e)

            @block.gpsimd
            def _(e):
                self.run("gpsimd", e)


def _reshape(ap, shape):
    if len(shape) == 1:
        return ap
    letters = "abcdefg"[: len(shape)]
    pat = "p ({}) -> p {}".format(" ".join(letters), " ".join(letters))
    return ap.rearrange(pat, **{letters[i]: shape[i] for i in range(len(shape))})


class Arena:
    def __init__(self, nc, name, n_elems, dtype):
        self.t = nc.alloc_sbuf_tensor(name, [128, n_elems], dtype)
        self.n = n_elems
        self.off = 0
        self.marks = []
        self.peak = 0

    def alloc(self, *shape):
        n = int(np.prod(shape))
        ap = self.t.ap()[:, self.off:self.off + n]
        self.off += n
        self.peak = max(self.peak, self.off)
        assert self.off <= self.n, f"arena overflow {self.off} > {self.n}"
        return _reshape(ap, shape)

    def push(self):
        self.marks.append(self.off)

    def pop(self):
        self.off = self.marks.pop()


def _nopi(eng):
    return eng.engine_nop() if hasattr(eng, "engine_nop") else eng.nop()



def host_consts():
    inv = 1.0 / (10000.0 ** (np.arange(0, HD, 2, dtype=np.float64) / HD))
    pos = np.arange(S, dtype=np.float64)
    fr = pos[:, None] * inv[None, :]
    fr = np.concatenate([fr, fr], axis=1)
    cos64 = np.concatenate([np.cos(fr).T] * 2, axis=0)
    sin64 = np.concatenate([np.sin(fr).T] * 2, axis=0)
    cos2 = np.ascontiguousarray(cos64.astype(np.float32))
    sin2 = np.ascontiguousarray(sin64.astype(np.float32))
    coslo = np.ascontiguousarray((cos64 - cos2.astype(np.float64)).astype(np.float32))
    sinlo = np.ascontiguousarray((sin64 - sin2.astype(np.float64)).astype(np.float32))

    R = np.zeros((64, 64), dtype=np.float32)
    for i in range(32):
        R[i, i + 32] = -1.0
        R[i + 32, i] = 1.0
    R2 = np.zeros((128, 128), dtype=np.float32)
    R2[:64, :64] = R
    R2[64:, 64:] = R
    rotT = np.ascontiguousarray(R2.T)

    idn = np.eye(128, dtype=np.float32)
    bde = np.zeros((128, E_TOT), dtype=np.float32)
    for p in range(128):
        bde[p, p // 16] = 1.0
    bdeT = np.ascontiguousarray(bde.T)
    return cos2, sin2, coslo, sinlo, rotT, idn, bde, bdeT



def build_fused(debug_taps=(), phases=("p3", "p4", "p4b", "p5")):
    nc = bass.Bass("TRN2", target_bir_lowering=False, debug=False)

    x_d = nc.dram_tensor("x", [S, D], F32, kind="ExternalInput").ap()
    wq_d = nc.dram_tensor("wq", [D, D], F32, kind="ExternalInput").ap()
    wkv_d = nc.dram_tensor("wkv", [D, 512], F32, kind="ExternalInput").ap()
    wo_d = nc.dram_tensor("wo", [D, D], F32, kind="ExternalInput").ap()
    bo_d = nc.dram_tensor("bo", [1, D], F32, kind="ExternalInput").ap()
    rwt_d = nc.dram_tensor("rwt", [D, E_TOT], F32, kind="ExternalInput").ap()
    w1_d = nc.dram_tensor("w1", [E_MINE, D, MOE_HID], F32, kind="ExternalInput").ap()
    b1_d = nc.dram_tensor("b1", [E_MINE, 128, MOE_HID // 128], F32, kind="ExternalInput").ap()
    w2_d = nc.dram_tensor("w2", [E_MINE, MOE_HID, D], F32, kind="ExternalInput").ap()
    b2_d = nc.dram_tensor("b2", [E_MINE, 1, D], F32, kind="ExternalInput").ap()
    cos_d = nc.dram_tensor("cos", [128, S], F32, kind="ExternalInput").ap()
    sin_d = nc.dram_tensor("sin", [128, S], F32, kind="ExternalInput").ap()
    coslo_d = nc.dram_tensor("coslo", [128, S], F32, kind="ExternalInput").ap()
    sinlo_d = nc.dram_tensor("sinlo", [128, S], F32, kind="ExternalInput").ap()
    rot_d = nc.dram_tensor("rot", [128, 128], F32, kind="ExternalInput").ap()
    idn_d = nc.dram_tensor("idn", [128, 128], F32, kind="ExternalInput").ap()
    bde_d = nc.dram_tensor("bde", [128, E_TOT], F32, kind="ExternalInput").ap()
    bdeT_d = nc.dram_tensor("bdeT", [E_TOT, 128], F32, kind="ExternalInput").ap()

    dattn_d = nc.dram_tensor("dattn", [HALF, D], F32, kind="Internal").ap()
    macc_d = nc.dram_tensor("macc", [S + 1, D], F32, kind="Internal").ap()
    xn_tm_d = nc.dram_tensor("xn_tm", [S + 1, D], BF16, kind="Internal").ap()
    sc_tm_d = nc.dram_tensor("sc_tm", [S + 1, 64], F32, kind="Internal").ap()
    scf_d = nc.dram_tensor("scf_d", [E_TOT, S], F32, kind="Internal").ap()
    idxd_d = nc.dram_tensor("idxd", [E_MINE, 10496], I32, kind="Internal").ap()
    utri_d = nc.dram_tensor("utri", [128, 128], F32, kind="ExternalInput").ap()
    onef_d = nc.dram_tensor("onef", [128, 128], F32, kind="ExternalInput").ap()
    tokid_d = nc.dram_tensor("tokid_in", [128, NT], I32, kind="ExternalInput").ap()
    xnat_d = nc.dram_tensor("xnat", [S, D], F32, kind="ExternalInput").ap()
    rowmap_d = nc.dram_tensor("rowmap", [128, NT], I32, kind="ExternalInput").ap()
    wq1_d = nc.dram_tensor("wq1", [D, D], F32, kind="ExternalInput").ap()
    wk1_d = nc.dram_tensor("wk1", [D, D], F32, kind="ExternalInput").ap()
    wv1_d = nc.dram_tensor("wv1", [D, D], F32, kind="ExternalInput").ap()
    wo1_d = nc.dram_tensor("wo1", [D, D], F32, kind="ExternalInput").ap()
    wg_d = nc.dram_tensor("wg", [D, L1_HID], I32, kind="ExternalInput").ap()
    wu_d = nc.dram_tensor("wu", [D, L1_HID], I32, kind="ExternalInput").ap()
    wd_d = nc.dram_tensor("wd", [L1_HID, D], I32, kind="ExternalInput").ap()
    tri_d = nc.dram_tensor("tri", [128, 128], F32, kind="ExternalInput").ap()
    pmask_d = nc.dram_tensor("pmask", [128, NT], F32, kind="ExternalInput").ap()
    out_d = nc.dram_tensor("out", [HALF, D], F32, kind="ExternalOutput").ap()
    x1_d = nc.dram_tensor("x1sc", [HALF, D], F32, kind="Internal").ap()
    attn1_d = nc.dram_tensor("attn1sp", [HALF, D], BF16, kind="Internal").ap()
    maccsum_d = nc.dram_tensor("maccsum", [S, D], F32, kind="Internal").ap()
    dattncat_d = nc.dram_tensor("dattncat", [S, D], F32, kind="Internal").ap()

    taps = {}

    def tap(name, shape, dtype=F32):
        if name in debug_taps:
            taps[name] = nc.dram_tensor("tap_" + name, list(shape), dtype,
                                        kind="ExternalOutput").ap()
            return taps[name]
        return None

    ctx = contextlib.ExitStack()
    s = Sched(nc, ctx)

    with ctx:
        bfA = Arena(nc, "bfA", 74300, BF16)
        fpA = Arena(nc, "fpA", 10400, F32)
        A = nc.alloc_sbuf_tensor

        idn = A("idn_s", [128, 128], F32)
        idnb = A("idnb", [128, 128], BF16)
        rott = A("rott", [128, 128], BF16)
        ones8 = A("ones8", [8, 1], F32)
        onesr = A("onesr", [1, E_TOT], F32)
        bde = A("bde_s", [128, E_TOT], F32)
        bdeT = A("bdeT_s", [E_TOT, 128], F32)
        sc_g = A("sc_g", [128, 256], F32)
        lo_t = A("lo_t", [8, 6], F32)
        thr = A("thr", [128, 1], F32)
        selv = A("selv", [128, 256], F32)
        mt_f = A("mt_f", [8, 128], F32)
        mtb = A("mtb", [128, NT, E_TOT], BF16)
        ctm = A("ctm", [128, NT, E_TOT], F32)
        sidx = A("sidx", [128, NT, E_TOT], I32)
        tokid = A("tokid", [128, NT], I32)
        stmp = A("stmp", [128, 2, E_TOT], F32)
        itmp = A("itmp", [128, 3, E_TOT], I32)
        idx_sb = A("idx_sb", [128, 9], I32)
        fill_t = A("fill_t", [128, 82], I32)
        utri = A("utri_s", [128, 128], BF16)
        onet = A("onet_s", [128, 128], BF16)
        zb = A("zb", [1, D], BF16)
        zf = A("zf", [1, 64], F32)
        deq_t = A("deq_t", [128, 2], F32)
        tri = A("tri_s", [128, 128], F32)
        pmask = A("pmask_s", [128, NT], F32)
        onesb = A("onesb", [128, 1], BF16)
        S_sb = A("S_sb", [128, NDC, 65], F32)
        S_bf = A("S_bf", [128, NDC, 65], BF16)
        rowmap = A("rowmap_s", [128, NT], I32)
        ln_m = A("ln_m", [128, 8], F32)
        r_sb = A("r_sb", [128, 1], F32)
        ones64 = A("ones64", [128, 64], F32)
        nshift = A("nshift", [128, 1], F32)
        gt_t = A("gt_t", [128, 9, 64], F32)
        xg_tm = A("xg_tm", [128, 2, D], BF16)
        b1_sb = A("b1_sb", [128, E_MINE, MOE_HID // 128], F32)

        AP_ = nc.alloc_psum_tensor
        psA = AP_("psA", [128, 512], F32)
        psB = AP_("psB", [128, 512], F32)
        psC = AP_("psC", [128, 512], F32)
        psD = AP_("psD", [128, 512], F32)
        psTa = AP_("psTa", [128, 1024], BF16)
        psTb = AP_("psTb", [128, 1024], BF16)
        psY = AP_("psY", [128, 1024], F32)

        dma = s.dma
        from concourse.bass_core_rust import IndirectOffsetOnAxis as _IOAx

        bfA.push()
        kfm = bfA.alloc(2, S)
        vtm = bfA.alloc(NT, 4 * 65)
        qfm = bfA.alloc(NDC, HALF)
        at_fm = bfA.alloc(NDC, HALF)
        fpA.push()
        zero_t = fpA.alloc(D)
        zero_t2 = _reshape(zero_t, (1, D))

        dma("sync", idn[:], idn_d, sem="cs0")
        dma("sync", bde[:], bde_d, sem="cs0")
        cs0v = dma("sync", bdeT[:], bdeT_d, sem="cs0")
        dma("gpsimd", idnb[:], idn_d, sem="cst")
        dma("gpsimd", rott[:], rot_d, sem="cst")
        for e in range(E_MINE):
            c0 = dma("sync", b1_sb[:, e, :], b1_d[e], sem="b1l")
        s.emit("vector", lambda e: e.memset(zero_t, 0.0), inc=("mi", 1))
        s.emit("vector", lambda e: e.memset(ones8[:], 1.0), inc=("mi", 1))
        s.emit("vector", lambda e: e.memset(onesr[:], 1.0), inc=("mi", 1))
        s.emit("vector", lambda e: e.memset(nshift[:], -EXP_SHIFT), inc=("mi", 1))
        s.emit("vector", lambda e: e.memset(fill_t[:], S), inc=("mi", 1))
        s.emit("vector", lambda e: e.memset(zb[:], 0.0), inc=("mi", 1))
        s.emit("vector", lambda e: e.memset(zf[:], 0.0), inc=("mi", 1))
        dma("gpsimd", utri[:], utri_d, sem="cst")
        cstv2 = dma("gpsimd", onet[:], onef_d, sem="cst")
        cstv = cstv2
        tiv = dma("sync", tokid[:], tokid_d, sem="tkl")
        zbv = ("mi", s.val("mi"))
        dma("scalar", xn_tm_d[S:S + 1, :], zb[:], sem="zrow", waits=[zbv])
        dma("scalar", sc_tm_d[S:S + 1, :], zf[:], sem="zrow", waits=[zbv])
        for kv2 in range(4):
            s.emit("vector", lambda e, kv2=kv2: e.memset(
                vtm[:, :, 65 * kv2 + 64:65 * kv2 + 65], 1.0), inc=("mi", 1))
        s.emit("vector", lambda e: e.memset(ones64[:], 1.0), inc=("mi", 1))
        zv = ("mi", s.val("mi"))
        for t in range(NT):
            dma("scalar", macc_d[128 * t:128 * (t + 1), :], zero_t,
                waits=[zv], sem="zacc")
        dma("scalar", macc_d[S:S + 1, :], zero_t[0:1, :], waits=[zv], sem="zacc")
        fpA.pop()

        fpA.push()
        scf = fpA.alloc(S)[0:8, :]
        bfA.push()
        fpA.push()
        xn_fm = bfA.alloc(NDC, HALF)
        xn_t = bfA.alloc(D)
        wq_sb = bfA.alloc(NDC, 128)
        wkvk_sb = bfA.alloc(NDC, 128)
        wkvv_sb = bfA.alloc(NDC, 256)
        raw_t = bfA.alloc(512)
        cs_t = bfA.alloc(2, 512)
        xt = fpA.alloc(2, D)
        sq_t = fpA.alloc(D)
        rpf = fpA.alloc(2, 512)
        xn_f = fpA.alloc(D)
        xnf_col = fpA.alloc(2, 128)
        rwtf = fpA.alloc(NDC, E_TOT)

        wvc = dma("gpsimd", wkvv_sb, wkv_d[:, 256:512].rearrange("(c p) o -> p c o", p=128),
                  sem="wc")
        rwv = dma("sync", rwtf, rwt_d.rearrange("(c p) o -> p c o", p=128),
                  sem="rwl")

        for half in range(2):
            col0 = HALF * half
            for tt in range(NTH):
                t = NTH * half + tt
                buf = tt % 2
                xin = dma("sync", xt[:, buf, :], x_d[128 * t:128 * (t + 1), :],
                          sem="xl" + str(buf),
                          waits=[("lnD", s.val("lnD") - 1),
                                 ("zacc", (NT + 1) * 16) if (half == 0 and tt == 0) else None])
                s.emit("vector", lambda e, b=buf: e.reduce_sum(
                    ln_m[:, 0:1], xt[:, b, :], axis=AX.X), waits=[xin])
                sv = s.emit("scalar", lambda e, b=buf: e.activation(
                    out=sq_t, in_=xt[:, b, :], func=AF.Square,
                    accum_out=ln_m[:, 1:2]),
                    waits=[xin, ("lnA", s.val("lnA"))], inc=("lnB", 1))

                def ln_stats(e):
                    e.tensor_scalar(ln_m[:, 2:3], ln_m[:, 0:1], 1.0 / D, None, ALU.mult)
                    e.tensor_scalar(ln_m[:, 3:4], ln_m[:, 1:2], 1.0 / D, None, ALU.mult)
                    e.tensor_tensor(ln_m[:, 5:6], ln_m[:, 2:3], ln_m[:, 2:3], ALU.mult)
                    e.tensor_tensor(ln_m[:, 6:7], ln_m[:, 3:4], ln_m[:, 5:6], ALU.subtract)
                    return e.tensor_scalar(ln_m[:, 6:7], ln_m[:, 6:7], 1e-5, None, ALU.add)
                av = s.emit("vector", ln_stats, waits=[("lnB", s.val("lnB"))],
                            inc=("lnA", 1))
                sv2 = s.emit("scalar", lambda e: e.activation(
                    out=ln_m[:, 7:8], in_=ln_m[:, 6:7], func=AF.Sqrt),
                    waits=[av, ("lnD", s.val("lnD"))], inc=("lnC", 1))

                def ln_norm(e, b=buf):
                    e.reciprocal(ln_m[:, 7:8], ln_m[:, 7:8])
                    e.tensor_scalar(xn_f, xt[:, b, :],
                                    ln_m[:, 2:3], ln_m[:, 7:8],
                                    ALU.subtract, ALU.mult)
                    return e.tensor_copy(xn_t, xn_f)
                lnv = s.emit("vector", ln_norm,
                             waits=[sv2, ("xns", s.val("xns")),
                                    ("tpe", s.val("tpe")),
                                    ("rtl", s.val("rtl"))], inc=("lnD", 1))
                s.emit("gpsimd", lambda e, t=t: e.indirect_dma_start(
                    xn_tm_d, _IOAx(tokid[:, t:t + 1], 0), xn_t, None),
                    waits=[lnv, tiv], inc=("xns_d", 16))
                s.emit("scalar", lambda e: _nopi(e),
                       waits=[("xns_d", s.val("xns_d"))], inc=("xns", 1))
                for dc in range(NDC):
                    pst = [psA, psB][dc % 2]
                    tv = s.emit("tensor", lambda e, dc=dc, p=pst: e.transpose(
                        p[:, 0:128], xn_f[:, 128 * dc:128 * (dc + 1)], idn[:]),
                        waits=[lnv, ("tpe", s.val("tpe") - 1), cs0v],
                        inc=("tpm", 1))

                    def evac2(e, dc=dc, tt=tt, p=pst):
                        e.tensor_copy(xn_fm[:, dc, 128 * tt:128 * (tt + 1)],
                                      p[:, 0:128])
                        return e.tensor_copy(xnf_col[:, dc % 2, :], p[:, 0:128])
                    s.emit("vector", evac2, waits=[tv, ("rtm", s.val("rtm"))],
                           inc=("tpe", 1))
                    s.emit("tensor", lambda e, dc=dc, t=t: e.matmul(
                        psD[0:8, 0:128], rwtf[:, dc, :], xnf_col[:, dc % 2, :],
                        start=(dc == 0), stop=(dc == NDC - 1)),
                        waits=[("tpe", s.val("tpe")), rwv] +
                              ([("rtl", s.val("rtl"))] if dc == 0 else []),
                        inc=("rtm", 1))
                s.emit("vector", lambda e, t=t: e.tensor_copy(
                    scf[:, 128 * t:128 * (t + 1)], psD[0:8, 0:128]),
                    waits=[("rtm", s.val("rtm"))], inc=("rtl", 1))
            lndone = ("tpe", s.val("tpe"))

            for oc in range(2):
                wvk = dma("gpsimd", wkvk_sb,
                          wkv_d[:, 128 * oc:128 * (oc + 1)].rearrange("(c p) o -> p c o", p=128),
                          sem="wkl",
                          waits=[("kpm", s.val("kpm")), ("qpm", s.val("qpm"))])
                for g in range(4):
                    mv = s.emit("tensor", lambda e, oc=oc, g=g: [e.matmul(
                        psB[:], wkvk_sb[:, dc, :],
                        xn_fm[:, dc, 512 * g:512 * (g + 1)],
                        start=(dc == 0), stop=(dc == NDC - 1)) for dc in range(NDC)][-1],
                        waits=[lndone, wvk, ("kpe", s.val("kpe")),
                               ("rtl", s.val("rtl"))], inc=("kpm", 1))
                    kv_ = s.emit("vector", lambda e, g=g: e.tensor_copy(
                        raw_t, psB[:]),
                        waits=[mv, ("rpe", s.val("rpe"))], inc=("kpe", 1))
                    dma("gpsimd", cs_t[:, 0, :],
                        cos_d[:, col0 + 512 * g:col0 + 512 * (g + 1)],
                        sem="csl", waits=[("rpe", s.val("rpe"))])
                    cvh = dma("gpsimd", cs_t[:, 1, :],
                              sin_d[:, col0 + 512 * g:col0 + 512 * (g + 1)], sem="csl")
                    s.emit("gpsimd", lambda e, g=g, col0=col0: e.dma_start(
                        out=cs_t[:, 0, :],
                        in_=coslo_d[:, col0 + 512 * g:col0 + 512 * (g + 1)],
                        accum_op=ALU.add), waits=[cvh], inc=("csl", 16))
                    cv2 = s.emit("gpsimd", lambda e, g=g, col0=col0: e.dma_start(
                        out=cs_t[:, 1, :],
                        in_=sinlo_d[:, col0 + 512 * g:col0 + 512 * (g + 1)],
                        accum_op=ALU.add), waits=[cvh], inc=("csl", 16))
                    rv = s.emit("tensor", lambda e, g=g: e.matmul(
                        psC[:], rott[:], raw_t, start=True, stop=True),
                        waits=[kv_, ("rpe", s.val("rpe")), cstv],
                        inc=("rpm", 1))

                    def rope_k(e, g=g, oc=oc, col0=col0):
                        lo = col0 + 512 * g
                        e.tensor_tensor(rpf[:, 0, :], psC[:], cs_t[:, 1, :], ALU.mult)
                        e.tensor_tensor(rpf[:, 1, :], raw_t, cs_t[:, 0, :], ALU.mult)
                        return e.tensor_tensor(kfm[:, oc, lo:lo + 512], rpf[:, 0, :],
                                               rpf[:, 1, :], ALU.add)
                    s.emit("vector", rope_k, waits=[rv, cv2], inc=("rpe", 1))

            for tt in range(NTH):
                t = NTH * half + tt
                mv = s.emit("tensor", lambda e, tt=tt: [e.matmul(
                    psC[:, 0:256], xn_fm[:, dc, 128 * tt:128 * (tt + 1)], wkvv_sb[:, dc, :],
                    start=(dc == 0), stop=(dc == NDC - 1)) for dc in range(NDC)][-1],
                    waits=[lndone, wvc, ("vpe", s.val("vpe")),
                           ("rpe", s.val("rpe"))], inc=("vpm", 1))

                def v_evac(e, t=t):
                    ins = None
                    for kv2 in range(4):
                        ins = e.tensor_copy(vtm[:, t, 65 * kv2:65 * kv2 + 64],
                                            psC[:, 64 * kv2:64 * (kv2 + 1)])
                    return ins
                s.emit("vector", v_evac, waits=[mv, zv], inc=("vpe", 1))

            if half == 0:
                for oc in range(NDC):
                    wvq = dma("gpsimd", wq_sb,
                              wq_d[:, 128 * oc:128 * (oc + 1)].rearrange("(c p) o -> p c o", p=128),
                              sem="wkl",
                              waits=[("qpm", s.val("qpm")), ("kpm", s.val("kpm"))])
                    for g in range(4):
                        mv = s.emit("tensor", lambda e, oc=oc, g=g: [e.matmul(
                            psD[:], wq_sb[:, dc, :],
                            xn_fm[:, dc, 512 * g:512 * (g + 1)],
                            start=(dc == 0), stop=(dc == NDC - 1)) for dc in range(NDC)][-1],
                            waits=[lndone, wvq, ("qpe", s.val("qpe")),
                                   ("rtl", s.val("rtl"))], inc=("qpm", 1))
                        qv_ = s.emit("vector", lambda e, g=g: e.tensor_copy(
                            raw_t, psD[:]),
                            waits=[mv, ("rpe", s.val("rpe"))], inc=("qpe", 1))
                        dma("gpsimd", cs_t[:, 0, :], cos_d[:, 512 * g:512 * (g + 1)],
                            sem="csl", waits=[("rpe", s.val("rpe"))])
                        cvh = dma("gpsimd", cs_t[:, 1, :], sin_d[:, 512 * g:512 * (g + 1)],
                                  sem="csl")
                        s.emit("gpsimd", lambda e, g=g: e.dma_start(
                            out=cs_t[:, 0, :], in_=coslo_d[:, 512 * g:512 * (g + 1)],
                            accum_op=ALU.add), waits=[cvh], inc=("csl", 16))
                        cv2 = s.emit("gpsimd", lambda e, g=g: e.dma_start(
                            out=cs_t[:, 1, :], in_=sinlo_d[:, 512 * g:512 * (g + 1)],
                            accum_op=ALU.add), waits=[cvh], inc=("csl", 16))
                        rv = s.emit("tensor", lambda e, g=g: e.matmul(
                            psC[:], rott[:], raw_t, start=True, stop=True),
                            waits=[qv_, ("rpe", s.val("rpe")),
                                   ("vpe", s.val("vpe"))], inc=("rpm", 1))

                        def rope_q(e, g=g, oc=oc):
                            e.tensor_tensor(rpf[:, 0, :], psC[:], cs_t[:, 1, :], ALU.mult)
                            e.tensor_tensor(rpf[:, 1, :], raw_t, cs_t[:, 0, :], ALU.mult)
                            return e.tensor_tensor(qfm[:, oc, 512 * g:512 * (g + 1)],
                                                   rpf[:, 0, :], rpf[:, 1, :], ALU.add)
                        s.emit("vector", rope_q, waits=[rv, cv2], inc=("rpe", 1))
            s.barrier()
        bfA.pop()
        fpA.pop()

        bv_ = ("bar", s.val("bar"))
        if tap("kfm", (128, 2, S), BF16) is not None:
            dma("sync", taps["kfm"], kfm, sem="tp0", waits=[bv_])
        if tap("qfm", (128, NDC, HALF), BF16) is not None:
            dma("sync", taps["qfm"], qfm, sem="tp0", waits=[bv_])
        if tap("vtm", (128, NT, 260), BF16) is not None:
            dma("sync", taps["vtm"], vtm, sem="tp0", waits=[bv_])
        if tap("logits", (8, S)) is not None:
            dma("sync", taps["logits"], scf[:], sem="tp0", waits=[bv_])

        if "p3" not in phases:
            fin0 = [("xns_d", s.val("xns_d"))]
            for nm in ("tp0",):
                if s.val(nm):
                    fin0.append((nm, s.val(nm)))
            s.emit("sync", lambda e: _nopi(e), waits=fin0, inc=("done", 1))
            s.finish()
            _finalize_libs(nc)
            return nc, taps
        fpA.push()
        o_sb = fpA.alloc(512)
        o_sb2 = _reshape(o_sb, (1, 512))

        sxv = s.emit("scalar", lambda e: e.activation(out=scf[:], in_=scf[:],
                                                      func=AF.Exp),
                     waits=[("tp0", s.val("tp0"))], inc=("sxm", 1))
        for g in range(8):
            mv = s.emit("tensor", lambda e, g=g: e.matmul(
                psA[0:1, :], ones8[:], scf[:, 512 * g:512 * (g + 1)],
                start=True, stop=True), waits=[sxv, ("sce", s.val("sce"))],
                inc=("zsm", 1))
            rv = s.emit("vector", lambda e: e.reciprocal(o_sb2[0:1, :], psA[0:1, :]),
                        waits=[mv], inc=("zre", 1))
            bv = s.emit("tensor", lambda e: e.matmul(
                psB[0:8, :], onesr[:], o_sb2[0:1, :], start=True, stop=True),
                waits=[rv], inc=("zbm", 1))
            s.emit("vector", lambda e, g=g: e.tensor_tensor(
                scf[:, 512 * g:512 * (g + 1)], scf[:, 512 * g:512 * (g + 1)],
                psB[0:8, :], ALU.mult), waits=[bv], inc=("sce", 1))
        scdone = ("sce", s.val("sce"))

        if tap("scf", (8, S)) is not None:
            dma("sync", taps["scf"], scf[:], sem="tp1", waits=[scdone])

        s.emit("vector", lambda e: e.memset(o_sb[:, 8:64], 0.0), inc=("stz", 1))
        for t in range(NT):
            tv = s.emit("tensor", lambda e, t=t: e.transpose(
                psC[0:128, 0:8], scf[:, 128 * t:128 * (t + 1)], idn[0:8, 0:8]),
                waits=[scdone, ("ste", s.val("ste")), cs0v], inc=("stm", 1))
            cv = s.emit("vector", lambda e: e.tensor_copy(o_sb[:, 0:8], psC[0:128, 0:8]),
                        waits=[tv, ("sts", s.val("sts")), ("stz", 1)],
                        inc=("ste", 1))
            s.emit("gpsimd", lambda e, t=t: e.indirect_dma_start(
                sc_tm_d, _IOAx(tokid[:, t:t + 1], 0), o_sb[:, 0:64], None),
                waits=[cv, tiv], inc=("sct_d", 16))
            s.emit("scalar", lambda e: _nopi(e),
                   waits=[("sct_d", s.val("sct_d"))], inc=("sts", 1))
        scfd = dma("sync", scf_d, scf[:], waits=[scdone], sem="sfd")
        for e in range(E_TOT):
            dma("sync", sc_g[16 * e:16 * e + 16, :],
                scf_d[e].rearrange("(p f) -> p f", p=16), waits=[scfd], sem="scg")
        scgv = ("scg", s.val("scg"))

        def bis_init(e):
            e.memset(lo_t[:, 0:1], 0.0)
            e.memset(lo_t[:, 1:2], 1.0)
            e.memset(lo_t[:, 2:3], 0.5)
            return e.memset(thr[:], 0.5)
        s.emit("vector", bis_init, waits=[scgv], inc=("bis", 1))
        for it in range(BISECT_ITERS):
            def cnt(e):
                e.tensor_scalar(selv[:], sc_g[:], thr[:], None, ALU.is_gt)
                return e.reduce_sum(ln_m[:, 0:1], selv[:], axis=AX.X)
            cv = s.emit("vector", cnt, waits=[("bis", s.val("bis"))], inc=("bis", 1))
            mv = s.emit("tensor", lambda e: e.matmul(
                psA[0:8, 0:1], bde[:], ln_m[:, 0:1], start=True, stop=True),
                waits=[cv, cs0v], inc=("bim", 1))

            def upd(e):
                e.tensor_scalar(lo_t[:, 3:4], psA[0:8, 0:1], float(CAP) - 0.5,
                                None, ALU.is_gt)
                e.tensor_tensor(lo_t[:, 4:5], lo_t[:, 2:3], lo_t[:, 0:1], ALU.subtract)
                e.tensor_tensor(lo_t[:, 4:5], lo_t[:, 4:5], lo_t[:, 3:4], ALU.mult)
                e.tensor_tensor(lo_t[:, 0:1], lo_t[:, 0:1], lo_t[:, 4:5], ALU.add)
                e.tensor_scalar(lo_t[:, 3:4], lo_t[:, 3:4], -1.0, 1.0,
                                ALU.mult, ALU.add)
                e.tensor_tensor(lo_t[:, 4:5], lo_t[:, 2:3], lo_t[:, 1:2], ALU.subtract)
                e.tensor_tensor(lo_t[:, 4:5], lo_t[:, 4:5], lo_t[:, 3:4], ALU.mult)
                e.tensor_tensor(lo_t[:, 1:2], lo_t[:, 1:2], lo_t[:, 4:5], ALU.add)
                e.tensor_tensor(lo_t[:, 2:3], lo_t[:, 0:1], lo_t[:, 1:2], ALU.add)
                return e.tensor_scalar(lo_t[:, 2:3], lo_t[:, 2:3], 0.5, None, ALU.mult)
            uv = s.emit("vector", upd, waits=[mv], inc=("bis", 1))
            last = (it == BISECT_ITERS - 1)
            bv = s.emit("tensor", lambda e, last=last: e.matmul(
                psB[:, 0:1], bdeT[:], lo_t[:, 0:1] if last else lo_t[:, 2:3],
                start=True, stop=True), waits=[uv], inc=("bim", 1))
            s.emit("vector", lambda e: e.tensor_copy(thr[:], psB[:, 0:1]),
                   waits=[bv], inc=("bis", 1))

        bisv = ("bis", s.val("bis"))
        for t in range(NT):
            mv0 = s.emit("vector", lambda e, t=t: e.tensor_scalar(
                mt_f[:], scf[:, 128 * t:128 * (t + 1)], lo_t[:, 0:1], None,
                ALU.is_gt), waits=[bisv, ("mtm", s.val("mtm"))], inc=("mte", 1))
            tv = s.emit("tensor", lambda e, t=t: e.transpose(
                psC[0:128, 0:8], mt_f[:], idn[0:8, 0:8]),
                waits=[mv0, ("mce", s.val("mce")), cs0v], inc=("mtm", 1))
            s.emit("vector", lambda e, t=t: e.tensor_copy(
                mtb[:, t, :], psC[0:128, 0:8]), waits=[tv], inc=("mce", 1))
        mdone = ("mce", s.val("mce"))
        for t in range(NT):
            mv = s.emit("tensor", lambda e, t=t: [e.matmul(
                psD[:, 0:8], utri[:] if tp == t else onet[:], mtb[:, tp, :],
                start=(tp == 0), stop=(tp == t)) for tp in range(t + 1)][-1],
                waits=[mdone, cstv2, ("cse", s.val("cse"))], inc=("csm", 1))
            s.emit("vector", lambda e, t=t: e.tensor_copy(
                ctm[:, t, :], psD[:, 0:8]), waits=[mv], inc=("cse", 1))
        cdone = ("cse", s.val("cse"))
        for t in range(NT):
            def mkidx(e, t=t):
                e.tensor_scalar(stmp[:, 0, :], ctm[:, t, :], 1.0, None, ALU.subtract)
                e.tensor_copy(itmp[:, 0, :], stmp[:, 0, :])
                e.tensor_scalar(itmp[:, 1, :], itmp[:, 0, :], 7, None,
                                ALU.arith_shift_right)
                e.tensor_scalar(itmp[:, 2, :], itmp[:, 1, :], 7, None,
                                ALU.arith_shift_left)
                e.tensor_tensor(itmp[:, 0, :], itmp[:, 0, :], itmp[:, 2, :],
                                ALU.subtract)
                e.tensor_scalar(itmp[:, 2, :], itmp[:, 0, :], 3, None,
                                ALU.arith_shift_left)
                e.tensor_tensor(itmp[:, 2, :], itmp[:, 2, :], itmp[:, 0, :], ALU.add)
                e.tensor_tensor(itmp[:, 2, :], itmp[:, 2, :], itmp[:, 1, :], ALU.add)
                e.tensor_copy(stmp[:, 0, :], itmp[:, 2, :])
                e.tensor_scalar(stmp[:, 0, :], stmp[:, 0, :], -10368.0 - t, None,
                                ALU.add)
                e.tensor_tensor(stmp[:, 1, :], mtb[:, t, :], stmp[:, 0, :], ALU.mult)
                e.tensor_scalar(stmp[:, 1, :], stmp[:, 1, :], 10368.0 + t, None,
                                ALU.add)
                return e.tensor_copy(sidx[:, t, :], stmp[:, 1, :])
            s.emit("vector", mkidx, waits=[cdone], inc=("sie", 1))
        sdone = ("sie", s.val("sie"))
        for e in range(E_MINE):
            dma("gpsimd", idxd_d[e:e + 1].rearrange("o (p c) -> (o p) c", p=128)[0:15, :],
                fill_t[0:15, :], sem="ifl", waits=[zv])
        iflv = ("ifl", s.val("ifl"))
        from concourse.bass_core_rust import IndirectOffsetOnAxis as _IOA
        if tap("thr", (128, 1)) is not None:
            dma("sync", taps["thr"], thr[:], sem="tp1", waits=[bisv])
        s.barrier()
        fpA.pop()
        fpA.pop()
        for t in range(NT):
            for e in range(E_MINE):
                s.emit("gpsimd", lambda eng, t=t, e=e: eng.indirect_dma_start(
                    idxd_d.rearrange("o (a b) -> (o a) b", b=1),
                    _IOA(sidx[:, t, e:e + 1], 0),
                    tokid[:, t:t + 1], None, element_offset=e * 10496),
                    waits=[sdone, iflv, tiv], inc=("isc", 16))
        iscv = ("isc", s.val("isc"))
        if tap("idxd", (E_MINE, 10496), I32) is not None:
            dma("sync", taps["idxd"], idxd_d, sem="tp1", waits=[iscv])

        if "p4" not in phases:
            fin0 = [("xns_d", s.val("xns_d")), ("sct_d", s.val("sct_d"))]
            for nm in ("tp0", "tp1"):
                if s.val(nm):
                    fin0.append((nm, s.val(nm)))
            s.emit("sync", lambda e: _nopi(e), waits=fin0, inc=("done", 1))
            s.finish()
            _finalize_libs(nc)
            return nc, taps
        bfA.push()
        fpA.push()
        pt_sb = bfA.alloc(4, 512)
        o65 = fpA.alloc(512)
        atq = bfA.alloc(512)
        psYb = psY[:, 512:1024]

        pend_ep = []
        n_odd = [0]

        def mk_epilogue(ov, par, kv, slot, qg):
            def doit():
                rv = s.emit("vector", lambda e: e.reciprocal(
                    o65[64:65, :], o65[64:65, :]),
                    waits=[ov], inc=("erc", 1))
                bv = s.emit("tensor", lambda e: e.matmul(
                    psY[0:64, 0:512], ones64[64:65, :],
                    o65[64:65, :], start=True, stop=True),
                    waits=[rv, ("epe", s.val("epe"))], inc=("epm", 1))
                dst = (at_fm[0:64, slot // 2, 512 * qg:512 * (qg + 1)]
                       if par == 0 else atq[0:64, :])
                epv = s.emit("vector", lambda e: e.tensor_tensor(
                    dst, o65[0:64, :], psY[0:64, 0:512], ALU.mult),
                    waits=[bv] + ([("afd", 16 * n_odd[0])] if par else []),
                    inc=("epe", 1))
                if par:
                    dma("sync", at_fm[64:128, slot // 2, 512 * qg:512 * (qg + 1)],
                        atq[0:64, :], sem="afd", waits=[epv])
                    n_odd[0] += 1
            return doit

        NP = NT // 2
        for slot in range(HEADS):
            head = SLOT_TO_HEAD[slot]
            kv = head // 4
            base = 64 * (slot % 2)
            kchunk = slot // 2
            kvchunk = kv // 2
            kvbase = 64 * (kv % 2)
            assert kvbase == base
            par = slot % 2
            for qg in range(4):
                q0 = 512 * qg
                GQ0 = s.val("qkm")

                def emit_pv2(p):
                    bp = (GQ0 + p) % 2
                    s0 = 2 * bp
                    s.emit("tensor", lambda e, p=p, kv=kv, s0=s0: [e.matmul(
                        psC[0:65, :], vtm[:, 2 * p + i, 65 * kv:65 * (kv + 1)],
                        pt_sb[:, s0 + i, :], start=(p == 0 and i == 0),
                        stop=(p == NP - 1 and i == 1)) for i in range(2)][-1],
                        waits=[("pxe", s.val("pxe") - (NP - 1 - p > 0))] +
                              ([("ove", s.val("ove"))] if p == 0 else []),
                        inc=("pvm", 1))
                for p in range(NP):
                    bp = (GQ0 + p) % 2
                    bnk = [(psA, psB), (psD, psYb)][bp]
                    s0 = 2 * bp
                    mv = s.emit("tensor", lambda e, p=p, bnk=bnk, kvchunk=kvchunk,
                                kvbase=kvbase, kchunk=kchunk, base=base, q0=q0: [e.matmul(
                        bnk[i][:],
                        kfm[kvbase:kvbase + 64, kvchunk,
                            128 * (2 * p + i):128 * (2 * p + i + 1)],
                        qfm[base:base + 64, kchunk, q0:q0 + 512],
                        start=True, stop=True) for i in range(2)][-1],
                        waits=[("pxe", s.val("pxe") - 1)], inc=("qkm", 1))
                    s.emit("scalar", lambda e, bnk=bnk, s0=s0: [e.activation(
                        out=pt_sb[:, s0 + i, :], in_=bnk[i][:], func=AF.Exp,
                        bias=nshift[:], scale=0.125) for i in range(2)][-1],
                        waits=[mv, ("pvm", s.val("pvm"))], inc=("pxe", 1),
                        nodrain=True)
                    if p == 6 and pend_ep:
                        pend_ep.pop()()
                    if p > 0:
                        emit_pv2(p - 1)
                emit_pv2(NP - 1)
                ov = s.emit("vector", lambda e: e.tensor_copy(
                    o65[0:65, :], psC[0:65, :]),
                    waits=[("pvm", s.val("pvm")), ("epe", s.val("epe"))],
                    inc=("ove", 1), nodrain=True)
                pend_ep.append(mk_epilogue(ov, par, kv, slot, qg))
        pend_ep.pop()()
        s.barrier()
        bfA.pop()
        fpA.pop()
        if tap("at_fm", (128, NDC, HALF), BF16) is not None:
            dma("sync", taps["at_fm"], at_fm, sem="tp2",
                waits=[("bar", s.val("bar"))])

        if "p4b" not in phases:
            fin0 = [("xns_d", s.val("xns_d")), ("sct_d", s.val("sct_d"))]
            for nm in ("tp0", "tp1", "tp2"):
                if s.val(nm):
                    fin0.append((nm, s.val(nm)))
            s.emit("sync", lambda e: _nopi(e), waits=fin0, inc=("done", 1))
            s.finish()
            _finalize_libs(nc)
            return nc, taps
        bfA.push()
        fpA.push()
        wo_sb = bfA.alloc(NDC, D)
        stage = fpA.alloc(2, D)
        bo_rep = fpA.alloc(D)
        bo_rep2 = _reshape(bo_rep, (1, D))

        bov = dma("sync", bo_rep2[0:1, :], bo_d)
        s.emit("gpsimd", lambda e: e.partition_broadcast(bo_rep2, bo_rep2[0:1, :]),
               waits=[bov], inc=("bob", 1))
        wov = dma("gpsimd", wo_sb, wo_d.rearrange("(c p) o -> p c o", p=128),
                  sem="wol", waits=[("afd", 16 * 32)])
        for t in range(NTH):
            tb = t % 2
            pY0 = [psA, psC][tb]
            pY1 = [psB, psD][tb]
            mv = s.emit("tensor", lambda e, t=t, pY0=pY0, pY1=pY1: [e.matmul(
                [pY0, pY1][hh][:], at_fm[:, ic, 128 * t:128 * (t + 1)],
                wo_sb[:, ic, 512 * hh:512 * (hh + 1)],
                start=(ic == 0), stop=(ic == NDC - 1))
                for ic in range(NDC) for hh in range(2)][-1],
                waits=[wov, ("woe", s.val("woe") - 1), ("afd", 16 * 32)],
                inc=("wom", 1))

            def wo_add(e, tb=tb, pY0=pY0, pY1=pY1):
                bo_f = _reshape(bo_rep, (D,))
                e.tensor_tensor(stage[:, tb, 0:512], pY0[:], bo_f[:, 0:512],
                                ALU.add)
                return e.tensor_tensor(stage[:, tb, 512:1024], pY1[:],
                                       bo_f[:, 512:1024], ALU.add)
            wv = s.emit("vector", wo_add,
                        waits=[mv, ("wos", s.val("wos") - 1), ("bob", 1)],
                        inc=("woe", 1))
            dma("scalar", dattn_d[128 * t:128 * (t + 1), :], stage[:, t % 2, :],
                waits=[wv], sem="wo_d" + str(t % 2))
            s.emit("scalar", lambda e: _nopi(e),
                   waits=[("wo_d" + str(t % 2), s.val("wo_d" + str(t % 2)))],
                   inc=("wos", 1))
        s.barrier()
        bfA.pop()
        fpA.pop()
        bfA.pop()

        if "p5" not in phases:
            fin0 = [("xns_d", s.val("xns_d")), ("sct_d", s.val("sct_d")),
                    ("wo_d0", s.val("wo_d0")), ("wo_d1", s.val("wo_d1"))]
            for nm in ("tp0", "tp1", "tp2"):
                if s.val(nm):
                    fin0.append((nm, s.val(nm)))
            s.emit("sync", lambda e: _nopi(e), waits=fin0, inc=("done", 1))
            s.finish()
            _finalize_libs(nc)
            return nc, taps
        bfA.push()
        fpA.push()
        xsel = bfA.alloc(NDC, CAP_PAD)
        h_sb = bfA.alloc(16, CAP_PAD)
        w1_sb = bfA.alloc(NDC, 2048)
        w2_sb = bfA.alloc(16, D)
        y_sb = fpA.alloc(9, D)
        b2_rep = fpA.alloc(D)
        b2_rep2 = _reshape(b2_rep, (1, D))

        from concourse.bass_core_rust import IndirectOffsetOnAxis as _IOA2
        for e in range(E_MINE):
            b2v = dma("sync", b2_rep2[0:1, :], b2_d[e],
                      sem="b2l", waits=[("gfin", s.val("gfin"))])
            s.emit("gpsimd", lambda eng: eng.partition_broadcast(
                b2_rep2, b2_rep2[0:1, :]), waits=[b2v], inc=("b2b", 1))

            ixv = dma("sync", idx_sb[:],
                      idxd_d[e:e + 1, 0:1152].rearrange("o (p c) -> (o p) c", p=128),
                      sem="ixl", waits=[iscv, ("mfr", s.val("mfr"))])
            for c in range(9):
                gb = c % 2
                gx = s.emit("gpsimd", lambda eng, c=c, gb=gb: eng.indirect_dma_start(
                    xg_tm[:, gb, :], None, xn_tm_d, _IOA2(idx_sb[:, c:c + 1], 0)),
                    waits=[ixv, ("xns", NT), ("zrow", 32),
                           ("gte", s.val("gte") - 8)], inc=("mgl" + str(gb), 16))
                gg = s.emit("gpsimd", lambda eng, c=c: eng.indirect_dma_start(
                    gt_t[:, c, :], None, sc_tm_d, _IOA2(idx_sb[:, c:c + 1], 0)),
                    waits=[ixv, ("sts", NT), ("gfin", s.val("gfin"))],
                    inc=("mgg", 16))
                for dc in range(NDC):
                    pst = [psTa, psTb][dc % 2]
                    tv = s.emit("tensor", lambda e2, c=c, dc=dc, gb=gb, p=pst: e2.transpose(
                        p[:, :128], xg_tm[:, gb, 128 * dc:128 * (dc + 1)], idnb[:]),
                        waits=[gx, ("gte", s.val("gte") - 1),
                               ("hsm", s.val("hsm"))], inc=("gtm", 1))
                    s.emit("vector", lambda e2, c=c, dc=dc, p=pst: e2.tensor_copy(
                        xsel[:, dc, 128 * c:128 * (c + 1)], p[:, :128]), nodrain=True,
                        waits=[tv], inc=("gte", 1))
            gl = ("gte", s.val("gte"))
            ggl = ("mgg", s.val("mgg"))
            if e == 0 and tap("xsel", (128, NDC, CAP_PAD), BF16) is not None:
                dma("sync", taps["xsel"], xsel, sem="tp3", waits=[gl])
            if e == 0 and tap("gt", (128, 9, 64)) is not None:
                dma("sync", taps["gt"], gt_t[:], sem="tp3", waits=[ggl])

            for hg in range(2):
                w1v = dma("gpsimd", w1_sb,
                          w1_d[e].rearrange("(c p) h -> p c h", p=128)[:, :, 2048 * hg:2048 * (hg + 1)],
                          sem="w1l", waits=[("hsm", s.val("hsm"))])
                w2v = dma("gpsimd", w2_sb,
                          w2_d[e, 2048 * hg:2048 * (hg + 1), :].rearrange("(c p) o -> p c o", p=128),
                          sem="w2l", waits=[("ymm", s.val("ymm"))])
                grp = 0
                for hc in range(16):
                    for (tg, tw) in ((0, 512), (512, 512), (1024, 128)):
                        pp = [psA, psB][grp % 2]
                        mv = s.emit("tensor", lambda e2, hc=hc, tg=tg, tw=tw, pp=pp: [
                            e2.matmul(pp[:, 0:tw], w1_sb[:, dc, 128 * hc:128 * (hc + 1)],
                                      xsel[:, dc, tg:tg + tw],
                                      start=(dc == 0), stop=(dc == NDC - 1))
                            for dc in range(NDC)][-1],
                            waits=[w1v, gl, ("hse", s.val("hse") - 1)],
                            inc=("hsm", 1))
                        s.emit("scalar", lambda e2, hc=hc, tg=tg, tw=tw, pp=pp, hg=hg, e=e: e2.activation(
                            out=h_sb[:, hc, tg:tg + tw], in_=pp[:, 0:tw], func=AF.Gelu,
                            bias=b1_sb[:, e, 16 * hg + hc:16 * hg + hc + 1], scale=1.0),
                            waits=[mv, c0], inc=("hse", 1), nodrain=True)
                        grp += 1
                hv = ("hse", s.val("hse"))
                for ts in range(9):
                    mv = s.emit("tensor", lambda e2, ts=ts: [
                        e2.matmul(psY[:, 512 * hh:512 * (hh + 1)],
                                  h_sb[:, hc, 128 * ts:128 * (ts + 1)],
                                  w2_sb[:, hc, 512 * hh:512 * (hh + 1)],
                                  start=(hc == 0), stop=(hc == 15))
                        for hc in range(16) for hh in range(2)][-1],
                        waits=[w2v, hv, ("yev", s.val("yev"))], inc=("ymm", 1))
                    if hg == 0:
                        s.emit("vector", lambda e2, ts=ts: e2.tensor_copy(
                            y_sb[:, ts, :], psY[:]),
                            waits=[mv, ("msc", s.val("msc"))], inc=("yev", 1))
                    else:
                        s.emit("vector", lambda e2, ts=ts: e2.tensor_tensor(
                            y_sb[:, ts, :], y_sb[:, ts, :], psY[:], ALU.add),
                            waits=[mv], inc=("yev", 1))
            yv = ("yev", s.val("yev"))

            def gate_fin(e2, e=e):
                ins = None
                for ts in range(9):
                    e2.tensor_tensor(y_sb[:, ts, :], y_sb[:, ts, :],
                                     _reshape(b2_rep, (D,)), ALU.add)
                    ins = e2.tensor_scalar(y_sb[:, ts, :], y_sb[:, ts, :],
                                           gt_t[:, ts, e:e + 1], None, ALU.mult)
                return ins
            gv = s.emit("vector", gate_fin, waits=[yv, ("b2b", s.val("b2b")), ggl],
                        inc=("gfin", 1))
            if e == 0 and tap("y0", (128, 9, D)) is not None:
                dma("sync", taps["y0"], y_sb, sem="tp3", waits=[gv])
            mscw = ("msc", s.val("msc"))
            for c in range(9):
                s.emit("gpsimd", lambda eng, e=e, c=c: eng.indirect_dma_start(
                    macc_d, _IOA2(idx_sb[:, c:c + 1], 0), y_sb[:, c, :], None,
                    compute_op=ALU.add),
                    waits=[gv, ("zacc", (NT + 1) * 16), mscw], inc=("msc", 16))
            s.emit("gpsimd", lambda e2: _nopi(e2), waits=[("msc", s.val("msc"))],
                   inc=("mfr", 1))
        bfA.pop()
        fpA.pop()

        RG = [[0, 1], [2, 3], [4, 5], [6, 7]]
        arv = s.emit("gpsimd", lambda e: e.collective_compute(
            "AllReduce", ALU.add, RG, [macc_d[0:S, :]], [maccsum_d]),
            waits=[("msc", s.val("msc")), ("zacc", (NT + 1) * 16)],
            inc=("ccr", 1))
        agv = s.emit("gpsimd", lambda e: e.collective_compute(
            "AllGather", ALU.bypass, RG, [dattn_d], [dattncat_d]),
            waits=[("wo_d0", s.val("wo_d0")), ("wo_d1", s.val("wo_d1"))],
            inc=("ccg", 1))

        dma("sync", tri[:], tri_d, sem="cs0b")
        b1pm = dma("sync", pmask[:], pmask_d, sem="cs0b")
        rmv = dma("sync", rowmap[:], rowmap_d, sem="rml")
        s.emit("vector", lambda e: e.memset(onesb[:], 1.0), inc=("mi", 1))
        s.emit("vector", lambda e: e.memset(S_sb[:], 0.0), inc=("mi", 1))

        def mk_deq(e):
            e.memset(deq_t[:, 0:1], 127.0)
            e.reciprocal(deq_t[:, 0:1], deq_t[:, 0:1])
            return e.tensor_tensor(deq_t[:, 1:2], deq_t[:, 0:1], deq_t[:, 0:1],
                                   ALU.mult)
        s.emit("vector", mk_deq, inc=("mi", 1))
        miv = ("mi", s.val("mi"))

        b_xnfm = bfA.alloc(NDC, S)
        bfA.push()
        b_qfm = bfA.alloc(NDC, HALF)
        b_ktm = bfA.alloc(D)
        b_vtm = bfA.alloc(D)
        b_kfmt = bfA.alloc(NDC, 128)
        b_xnt = bfA.alloc(D)
        b_att = bfA.alloc(2, D)

        fpA.push()
        b_xt = fpA.alloc(4, D)
        b_sqt = fpA.alloc(D)

        for t in range(NT):
            w0 = [("lnD", s.val("lnD")), ("x1s0", s.val("x1s0")),
                  ("x1s1", s.val("x1s1")), rmv, arv, agv]
            l0 = s.emit("gpsimd", lambda e, t=t: e.indirect_dma_start(
                b_xt[:, 0, :], None, xnat_d, _IOA2(rowmap[:, t:t + 1], 0)),
                waits=w0, inc=("bxl0", 16))
            l1 = s.emit("gpsimd", lambda e, t=t: e.indirect_dma_start(
                b_xt[:, 1, :], None, dattncat_d, _IOA2(rowmap[:, t:t + 1], 0)),
                waits=w0, inc=("bxl1", 16))
            l2 = s.emit("gpsimd", lambda e, t=t: e.indirect_dma_start(
                b_xt[:, 2, :], None, maccsum_d, _IOA2(rowmap[:, t:t + 1], 0)),
                waits=w0, inc=("bxl2", 16))

            def assemble(e):
                e.tensor_tensor(b_xt[:, 0, :], b_xt[:, 0, :], b_xt[:, 1, :], ALU.add)
                e.tensor_tensor(b_xt[:, 0, :], b_xt[:, 0, :], b_xt[:, 2, :], ALU.add)
                return e.reduce_sum(ln_m[:, 0:1], b_xt[:, 0, :], axis=AX.X)
            asv = s.emit("vector", assemble, waits=[l0, l1, l2], inc=("asm", 1))
            if t >= NTH:
                dma("scalar", x1_d[128 * (t - NTH):128 * (t - NTH + 1), :],
                    b_xt[:, 0, :], sem="x1s" + str(t % 2), waits=[asv])
            sv = s.emit("scalar", lambda e: e.activation(
                out=b_sqt, in_=b_xt[:, 0, :], func=AF.Square, accum_out=ln_m[:, 1:2]),
                waits=[asv, ("lnA", s.val("lnA"))], inc=("lnB", 1))

            def ln_stats(e):
                e.tensor_scalar(ln_m[:, 2:3], ln_m[:, 0:1], 1.0 / D, None, ALU.mult)
                e.tensor_scalar(ln_m[:, 3:4], ln_m[:, 1:2], 1.0 / D, None, ALU.mult)
                e.tensor_tensor(ln_m[:, 5:6], ln_m[:, 2:3], ln_m[:, 2:3], ALU.mult)
                e.tensor_tensor(ln_m[:, 6:7], ln_m[:, 3:4], ln_m[:, 5:6], ALU.subtract)
                return e.tensor_scalar(ln_m[:, 6:7], ln_m[:, 6:7], 1e-5, None, ALU.add)
            av = s.emit("vector", ln_stats, waits=[("lnB", s.val("lnB"))], inc=("lnA", 1))
            sv2 = s.emit("scalar", lambda e: e.activation(
                out=ln_m[:, 7:8], in_=ln_m[:, 6:7], func=AF.Sqrt),
                waits=[av, ("lnD", s.val("lnD"))], inc=("lnC", 1))

            def ln_norm(e):
                e.reciprocal(ln_m[:, 7:8], ln_m[:, 7:8])
                return e.tensor_scalar(b_xnt, b_xt[:, 0, :], ln_m[:, 2:3], ln_m[:, 7:8],
                                       ALU.subtract, ALU.mult)
            lnv = s.emit("vector", ln_norm, waits=[sv2, ("tpe", s.val("tpe"))],
                         inc=("lnD", 1))
            for dc in range(NDC):
                pst = [psTa, psTb][dc % 2]
                tv = s.emit("tensor", lambda e, dc=dc, p=pst: e.transpose(
                    p[:, :128], b_xnt[:, 128 * dc:128 * (dc + 1)], idnb[:]),
                    waits=[lnv, ("tpe", s.val("tpe") - 1), cstv], inc=("tpm", 1))
                s.emit("vector", lambda e, dc=dc, t=t, p=pst: e.tensor_copy(
                    b_xnfm[:, dc, 128 * t:128 * (t + 1)], p[:, :128]),
                    waits=[tv], inc=("tpe", 1), nodrain=True)
        s.barrier()
        fpA.pop()

        fpA.push()
        el_t = fpA.alloc(2, D)
        bfA.push()
        b_wq = bfA.alloc(NDC, D)

        wqv = dma("gpsimd", b_wq, wq1_d.rearrange("(c p) o -> p c o", p=128), sem="wql")

        for oc in range(NDC):
            for g in range(4):
                mv = s.emit("tensor", lambda e, oc=oc, g=g: [e.matmul(
                    psD[:], b_wq[:, dc, 128 * oc:128 * (oc + 1)],
                    b_xnfm[:, dc, HALF + 512 * g:HALF + 512 * (g + 1)],
                    start=(dc == 0), stop=(dc == NDC - 1)) for dc in range(NDC)][-1],
                    waits=[wqv, ("qpe", s.val("qpe"))], inc=("qpm", 1))

                def q_elu(e, oc=oc, g=g):
                    lo = 512 * g
                    e.tensor_scalar(el_t[:, 0, 0:512], psD[:], 0.0, None, ALU.min)
                    return e.tensor_copy(b_qfm[:, oc, lo:lo + 512], psD[:])
                ev1 = s.emit("vector", q_elu,
                             waits=[mv, ("qel", s.val("qel")),
                                    ("x1s0", NTH // 2 * 16), ("x1s1", NTH // 2 * 16)],
                             inc=("qpe", 1))
                ev2 = s.emit("scalar", lambda e: e.activation(
                    out=el_t[:, 1, 0:512], in_=el_t[:, 0, 0:512], func=AF.Exp),
                    waits=[ev1], inc=("qex", 1), nodrain=True)
                s.emit("vector", lambda e, oc=oc, g=g: e.scalar_tensor_tensor(
                    b_qfm[:, oc, 512 * g:512 * (g + 1)], b_qfm[:, oc, 512 * g:512 * (g + 1)],
                    0.0, el_t[:, 1, 0:512], ALU.max, ALU.add),
                    waits=[ev2], inc=("qel", 1))
        s.barrier()
        bfA.pop()

        bfA.push()
        wk_sb = bfA.alloc(NDC, D)
        wv_sb = bfA.alloc(NDC, D)
        r8 = fpA.alloc(16)
        wkv_ = dma("gpsimd", wk_sb, wk1_d.rearrange("(c p) o -> p c o", p=128), sem="wkl")
        wvv = dma("gpsimd", wv_sb, wv1_d.rearrange("(c p) o -> p c o", p=128), sem="wvl")
        for c in range(NT):
            kmv = s.emit("tensor", lambda e, c=c: [e.matmul(
                psY[:, 512 * hh:512 * (hh + 1)], b_xnfm[:, dc, 128 * c:128 * (c + 1)],
                wk_sb[:, dc, 512 * hh:512 * (hh + 1)],
                start=(dc == 0), stop=(dc == NDC - 1))
                for dc in range(NDC) for hh in range(2)][-1],
                waits=[wkv_, ("kue", s.val("kue"))], inc=("kpm", 1))

            def k_elu_a(e, c=c):
                e.tensor_scalar(el_t[:, 0, :], psY[:], 0.0, None, ALU.min)
                return e.tensor_copy(b_ktm, psY[:])
            k1 = s.emit("vector", k_elu_a,
                        waits=[kmv, ("fte", s.val("fte")), ("sde", s.val("sde"))],
                        inc=("kue", 1))
            k2 = s.emit("scalar", lambda e: e.activation(
                out=el_t[:, 1, :], in_=el_t[:, 0, :], func=AF.Exp),
                waits=[k1], inc=("kex", 1), nodrain=True)

            def k_elu_b(e, c=c):
                e.scalar_tensor_tensor(b_ktm, b_ktm, 0.0, el_t[:, 1, :], ALU.max, ALU.add)
                return e.tensor_scalar(b_ktm, b_ktm, pmask[:, c:c + 1], None, ALU.mult)
            k3 = s.emit("vector", k_elu_b, waits=[k2, b1pm], inc=("kue", 1))
            vmv = s.emit("tensor", lambda e, c=c: [e.matmul(
                [psA, psB][hh][:], b_xnfm[:, dc, 128 * c:128 * (c + 1)],
                wv_sb[:, dc, 512 * hh:512 * (hh + 1)],
                start=(dc == 0), stop=(dc == NDC - 1))
                for dc in range(NDC) for hh in range(2)][-1],
                waits=[wvv, ("vpe", s.val("vpe")), ("sma", s.val("sma")),
                       ("eda", s.val("eda")), ("ssu", s.val("ssu"))],
                inc=("vpm", 1))

            def v_evac(e):
                e.tensor_copy(b_vtm[:, 0:512], psA[:])
                return e.tensor_copy(b_vtm[:, 512:1024], psB[:])
            v1 = s.emit("vector", v_evac,
                        waits=[vmv, ("spv", s.val("spv")), ("sde", s.val("sde"))],
                        inc=("vpe", 1), nodrain=True)
            if c >= NTH:
                for dc in range(NDC):
                    pst = [psTa, psTb][dc % 2]
                    tv = s.emit("tensor", lambda e, dc=dc, p=pst: e.transpose(
                        p[:, :128], b_ktm[:, 128 * dc:128 * (dc + 1)], idnb[:]),
                        waits=[k3, cstv, ("fte", s.val("fte") - 1),
                               ("sam", s.val("sam"))], inc=("ftm", 1))
                    s.emit("vector", lambda e, dc=dc, p=pst: e.tensor_copy(
                        b_kfmt[:, dc, :], p[:, :128]), waits=[tv], inc=("fte", 1),
                        nodrain=True)
                kfv = ("fte", s.val("fte"))
                cq = 128 * (c - NTH)

                sbv = s.emit("vector", lambda e: [e.tensor_copy(
                    S_bf[:, f2, :], S_sb[:, f2, :]) for f2 in range(8)][-1],
                    waits=[("ssu", s.val("ssu"))], inc=("sbf", 1))

                for hg2 in range(2):
                    hs = 8 * hg2

                    def inter_f(e, hs, cq, par2):
                        ins = None
                        for lh in range(par2, 8, 2):
                            h = hs + lh
                            b2 = 64 * (h % 2)
                            f2 = h // 2
                            ins = e.matmul(psC[0:128, 64 * lh:64 * lh + 64],
                                           b_qfm[b2:b2 + 64, f2, cq:cq + 128],
                                           S_bf[b2:b2 + 64, f2, 0:64],
                                           start=(lh == 0), stop=False)
                            ins = e.matmul(psB[0:128, h:h + 1],
                                           b_qfm[b2:b2 + 64, f2, cq:cq + 128],
                                           S_bf[b2:b2 + 64, f2, 64:65],
                                           start=(lh == 0), stop=False)
                        return ins
                    iv0 = s.emit("tensor",
                                 lambda e, hs=hs, cq=cq: inter_f(e, hs, cq, 0),
                                 waits=[sbv, v1, ("epv", s.val("epv")),
                                        ("eda", s.val("eda")),
                                        ("bar", s.val("bar"))], inc=("sim_", 1))
                    iv = s.emit("tensor",
                                lambda e, hs=hs, cq=cq: inter_f(e, hs, cq, 1),
                                waits=[iv0], inc=("sim_", 1))

                    mstep = []
                    for par2 in range(2):
                        pbank = [psD, psA][par2]

                        def at_f(e, hs=hs, par2=par2, pbank=pbank, cq=cq):
                            ins = None
                            for j in range(4):
                                lh = 2 * j + par2
                                h = hs + lh
                                b2 = 64 * (h % 2)
                                f2 = h // 2
                                ins = e.matmul(pbank[:, 128 * j:128 * (j + 1)],
                                               b_kfmt[b2:b2 + 64, f2, :],
                                               b_qfm[b2:b2 + 64, f2, cq:cq + 128],
                                               start=True, stop=True)
                            return ins
                        wts = [kfv, ("sma", s.val("sma"))]
                        if par2 == 1:
                            wts.append(v1)
                        av_ = s.emit("tensor", at_f, waits=wts, inc=("sam", 1))
                        mv_ = s.emit("vector", lambda e, par2=par2, pbank=pbank: [
                            e.tensor_tensor(
                                b_xnt[:, 128 * (2 * j + par2):128 * (2 * j + par2 + 1)],
                                pbank[:, 128 * j:128 * (j + 1)], tri[:], ALU.mult)
                            for j in range(4)][-1],
                            waits=[av_, b1pm, ("spv", s.val("spv"))],
                            inc=("sma", 1), nodrain=True)
                        mstep.append(mv_)

                    def intra_f(e, hs=hs):
                        ins = None
                        for lh in range(8):
                            h = hs + lh
                            xs = b_xnt[:, 128 * lh:128 * (lh + 1)]
                            ins = e.matmul(psC[0:128, 64 * lh:64 * lh + 64], xs,
                                           b_vtm[:, 64 * h:64 * (h + 1)],
                                           start=False, stop=(lh == 7))
                            ins = e.matmul(psB[0:128, h:h + 1], xs, onesb[:],
                                           start=False, stop=(lh == 7))
                        return ins
                    i5 = s.emit("tensor", intra_f, waits=[mstep[0], mstep[1], iv],
                                inc=("spv", 1))

                    e1 = s.emit("vector", lambda e, hs=hs: e.tensor_scalar(
                        r8[:, hs:hs + 8], psB[0:128, hs:hs + 8], 1e-6, None,
                        ALU.add), waits=[i5], inc=("eda", 1))
                    e2 = s.emit("vector", lambda e, hs=hs: e.reciprocal(
                        r8[:, hs:hs + 8], r8[:, hs:hs + 8]),
                        waits=[e1], inc=("edb", 1))
                    s.emit("vector", lambda e, hs=hs, c=c: [e.tensor_scalar(
                        b_att[:, c % 2, 64 * (hs + lh):64 * (hs + lh + 1)],
                        psC[0:128, 64 * lh:64 * (lh + 1)], r8[:, hs + lh:hs + lh + 1],
                        None, ALU.mult) for lh in range(8)][-1],
                        waits=[e2, ("ats" + str(c % 2), s.val("ats" + str(c % 2)))],
                        inc=("epv", 1))

            for hg2 in range(2):
                hs = 8 * hg2

                def upd_f(e, hs=hs):
                    ins = None
                    for lh in range(8):
                        h = hs + lh
                        b2 = 64 * (h % 2)
                        col = 16 + 65 * (lh // 2)
                        ins = e.matmul(psB[b2:b2 + 64, col:col + 64],
                                       b_ktm[:, 64 * h:64 * (h + 1)],
                                       b_vtm[:, 64 * h:64 * (h + 1)],
                                       start=True, stop=False)
                        ins = e.matmul(psB[b2:b2 + 64, col + 64:col + 65],
                                       b_ktm[:, 64 * h:64 * (h + 1)], onesb[:],
                                       start=False, stop=True)
                    return ins
                u1 = s.emit("tensor", upd_f,
                            waits=[k3, v1, ("ssu", s.val("ssu")),
                                   ("eda", s.val("eda"))], inc=("sde", 1))
                s.emit("vector", lambda e, hs=hs: [e.tensor_tensor(
                    S_sb[:, 4 * hg3 + f2, :], S_sb[:, 4 * hg3 + f2, :],
                    psB[:, 16 + 65 * f2:16 + 65 * f2 + 65], ALU.add)
                    for hg3 in [hs // 8] for f2 in range(4)][-1],
                    waits=[u1], inc=("ssu", 1))
            if c >= NTH:
                dma("scalar", attn1_d[128 * (c - NTH):128 * (c - NTH + 1), :],
                    b_att[:, c % 2, :], sem="ats" + str(c % 2),
                    waits=[("epv", s.val("epv"))])
        s.barrier()
        bfA.pop()
        fpA.pop()
        bfA.pop()

        bfA.push()
        fpA.push()
        b_atfm = bfA.alloc(2, NDC, 128)
        at_ld = bfA.alloc(2, D)
        b_wo = bfA.alloc(NDC, D)
        x1t = fpA.alloc(2, D)
        wov = dma("gpsimd", b_wo, wo1_d.rearrange("(c p) o -> p c o", p=128), sem="bwol")
        for t in range(NTH):
            tb = t % 2
            atv = dma("sync", at_ld[:, tb, :], attn1_d[128 * t:128 * (t + 1), :],
                      sem="atl" + str(tb),
                      waits=[("fte", s.val("fte") - 8),
                             ("ats0", NTH // 2 * 16), ("ats1", NTH // 2 * 16)])
            x1v = dma("scalar", x1t[:, tb, :], x1_d[128 * t:128 * (t + 1), :],
                      sem="x1l" + str(tb),
                      waits=[("woe", s.val("woe") - 1),
                             ("os" + str(tb), s.val("os" + str(tb))),
                             ("x1s0", NTH // 2 * 16), ("x1s1", NTH // 2 * 16)])
            for dc in range(NDC):
                pst = [psTa, psTb][dc % 2]
                tv = s.emit("tensor", lambda e, tb=tb, dc=dc, p=pst: e.transpose(
                    p[:, :128], at_ld[:, tb, 128 * dc:128 * (dc + 1)], idnb[:]),
                    waits=[atv, ("fte", s.val("fte") - 1),
                           ("wom", s.val("wom") - 1)],
                    inc=("ftm", 1))
                s.emit("vector", lambda e, tb=tb, dc=dc, p=pst: e.tensor_copy(
                    b_atfm[:, tb, dc, :], p[:, :128]), waits=[tv], inc=("fte", 1),
                    nodrain=True)
            mv = s.emit("tensor", lambda e, tb=tb: [e.matmul(
                psY[:, 512 * hh:512 * (hh + 1)], b_atfm[:, tb, ic, :],
                b_wo[:, ic, 512 * hh:512 * (hh + 1)],
                start=(ic == 0), stop=(ic == NDC - 1))
                for ic in range(NDC) for hh in range(2)][-1],
                waits=[("fte", s.val("fte")), wov, ("woe", s.val("woe"))],
                inc=("wom", 1))
            wv = s.emit("vector", lambda e, tb=tb: e.tensor_tensor(
                x1t[:, tb, :], x1t[:, tb, :], psY[:], ALU.add),
                waits=[mv, x1v], inc=("woe", 1))
            dma("scalar", out_d[128 * t:128 * (t + 1), :], x1t[:, tb, :],
                waits=[wv], sem="os" + str(tb))
        s.barrier()
        bfA.pop()
        fpA.pop()

        bfA.push()
        fpA.push()
        wgt = bfA.alloc(2, NDC, 128)
        wut = bfA.alloc(2, NDC, 128)
        wd_sb = bfA.alloc(12, D)
        hact = bfA.alloc(12, 2048)
        sg_t = bfA.alloc(512)
        yst = fpA.alloc(2, D)

        gdone = {}
        udone = {}
        it = 0
        dn_h0 = 0
        for hh2 in range(2):
            for hc in range(12):
                hcg = 12 * hh2 + hc
                hb = hcg % 2
                g1 = dma("gpsimd", wgt[:, hb],
                         wg_d.rearrange("(c p) o -> p c o", p=128)[:, :, 128 * hcg:128 * (hcg + 1)],
                         sem="wgl" + str(hb),
                         waits=([gdone[hcg - 2]] if hcg - 2 in gdone else []))
                g2 = dma("gpsimd", wut[:, hb],
                         wu_d.rearrange("(c p) o -> p c o", p=128)[:, :, 128 * hcg:128 * (hcg + 1)],
                         sem="wul" + str(hb),
                         waits=([udone[hcg - 2]] if hcg - 2 in udone else []))
                dma("gpsimd", wd_sb[:, hc, :], wd_d[128 * hcg:128 * (hcg + 1), :],
                    sem="wdl0",
                    waits=([("dnm", dn_h0)] if hh2 == 1 else []))
                for tg in range(4):
                    pg = [psA, psB][it % 2]
                    pu = [psC, psD][it % 2]
                    tg0 = HALF + 512 * tg
                    mv = s.emit("tensor", lambda e, hb=hb, tg0=tg0, pg=pg: [e.matmul(
                        pg[:], wgt[:, hb, dc, :], b_xnfm[:, dc, tg0:tg0 + 512],
                        start=(dc == 0), stop=(dc == NDC - 1)) for dc in range(NDC)][-1],
                        waits=[g1, ("sge", s.val("sge") - 1)], inc=("gm_", 1))
                    mv2 = s.emit("tensor", lambda e, hb=hb, tg0=tg0, pu=pu: [e.matmul(
                        pu[:], wut[:, hb, dc, :], b_xnfm[:, dc, tg0:tg0 + 512],
                        start=(dc == 0), stop=(dc == NDC - 1)) for dc in range(NDC)][-1],
                        waits=[g2, ("hae", s.val("hae") - 1)], inc=("gum", 1))
                    sv = s.emit("scalar", lambda e, pg=pg: e.activation(
                        out=sg_t, in_=pg[:], func=AF.Silu, scale=deq_t[:, 0:1]),
                        waits=[mv, ("hae", s.val("hae")), miv], inc=("sge", 1),
                        nodrain=True)
                    wmul = [sv, mv2]
                    if hh2 == 1:
                        wmul.append(("dnm", dn_h0))
                    s.emit("vector", lambda e, hc=hc, tg=tg, pu=pu: e.tensor_tensor(
                        hact[:, hc, 512 * tg:512 * (tg + 1)], sg_t, pu[:], ALU.mult),
                        waits=wmul, inc=("hae", 1))
                    it += 1
                gdone[hcg] = mv
                udone[hcg] = mv2
            hdone = ("hae", s.val("hae"))
            for tc in range(16):
                tb = tc % 2
                dn = s.emit("tensor", lambda e, tc=tc: [
                    e.matmul(psY[:, 512 * hh:512 * (hh + 1)],
                             hact[:, hc, 128 * tc:128 * (tc + 1)],
                             wd_sb[:, hc, 512 * hh:512 * (hh + 1)],
                             start=(hc == 0), stop=(hc == 11))
                    for hc in range(12) for hh in range(2)][-1],
                    waits=[("wdl0", 16 * 12 * (hh2 + 1)), hdone,
                           ("dne", s.val("dne"))], inc=("dnm", 1))
                yv = s.emit("vector", lambda e, tb=tb: e.tensor_scalar(
                    yst[:, tb, :], psY[:], deq_t[:, 1:2], None, ALU.mult),
                    waits=[dn, ("ya" + str(tb), s.val("ya" + str(tb))),
                           ("os0", NTH // 2 * 16), ("os1", NTH // 2 * 16)],
                    inc=("dne", 1))
                av = s.emit("gpsimd", lambda e, tc=tc, tb=tb: e.dma_start(
                    out=out_d[128 * tc:128 * (tc + 1), :], in_=yst[:, tb, :],
                    accum_op=ALU.add),
                    waits=[yv, ("os0", NTH // 2 * 16), ("os1", NTH // 2 * 16)],
                    inc=("ya" + str(tb), 16))
            dn_h0 = s.val("dnm")
        bfA.pop()
        fpA.pop()

        fin = [("ya0", s.val("ya0")), ("ya1", s.val("ya1")),
               ("os0", s.val("os0")), ("os1", s.val("os1"))]
        for nm in ("tp0", "tp1", "tp2", "tp3"):
            if s.val(nm):
                fin.append((nm, s.val(nm)))
        s.emit("sync", lambda e: _nopi(e), waits=fin, inc=("done", 1))


    s.finish()
    _finalize_libs(nc)
    import os as _os
    if _os.environ.get("KDBG_SEMS"):
        print("SEMS:", len(s.handles), sorted(s.handles))
    return nc, taps



_CACHE = {}


def _prep_fused_inputs(inputs):
    cos2, sin2, coslo, sinlo, rotT, idn, bde, bdeT = host_consts()
    x = np.ascontiguousarray(np.asarray(inputs["L0_attn_x"], dtype=np.float32))
    wq = np.asarray(inputs["L0_attn_Wq"], dtype=np.float32)
    wkv = np.ascontiguousarray(np.asarray(inputs["L0_attn_Wkv"], dtype=np.float32))
    wo = np.asarray(inputs["L0_attn_Wo"], dtype=np.float32)
    bo = np.ascontiguousarray(np.asarray(inputs["L0_attn_bo"], dtype=np.float32).reshape(1, D))
    rw = np.asarray(inputs["L0_ffn_router_w"], dtype=np.float32)
    w1 = np.asarray(inputs["L0_ffn_w1"], dtype=np.float32)
    b1 = np.asarray(inputs["L0_ffn_b1"], dtype=np.float32)
    w2 = np.asarray(inputs["L0_ffn_w2"], dtype=np.float32)
    b2 = np.asarray(inputs["L0_ffn_b2"], dtype=np.float32)

    wq_p = np.empty_like(wq)
    wo_p = np.empty_like(wo)
    for h in range(HEADS):
        sl = SIGMA[h]
        wq_p[:, 64 * sl:64 * (sl + 1)] = wq[:, 64 * h:64 * (h + 1)]
        wo_p[64 * sl:64 * (sl + 1), :] = wo[64 * h:64 * (h + 1), :]
    wq_p = np.ascontiguousarray(wq_p)
    wo_p = np.ascontiguousarray(wo_p)
    utri = np.ascontiguousarray(np.triu(np.ones((128, 128), np.float32)))
    onef = np.ones((128, 128), np.float32)
    tri = np.ascontiguousarray(np.triu(np.ones((128, 128), np.float32)))
    wg = np.ascontiguousarray(np.asarray(inputs["L1_ffn_w_gate"], np.int32).T)
    wu = np.ascontiguousarray(np.asarray(inputs["L1_ffn_w_up"], np.int32).T)
    wd = np.ascontiguousarray(np.asarray(inputs["L1_ffn_w_down"], np.int32).T)

    in_maps = []
    for c in range(8):
        b = c // 2
        hf = c % 2
        ordr = (np.concatenate([np.arange(HALF, S), np.arange(0, HALF)])
                if hf == 1 else np.arange(S))
        myex = [0, 1, 2, 3] if hf == 0 else [4, 5, 6, 7]
        exord = myex + [e for e in range(E_TOT) if e not in myex]
        tokid = ((np.arange(S) + hf * HALF) % S).astype(np.int32)
        tokid = np.ascontiguousarray(tokid.reshape(NT, 128).T)
        if hf == 0:
            rmap = np.concatenate([np.zeros(HALF, np.int64), np.arange(0, HALF)])
            pm = np.concatenate([np.zeros((128, NTH), np.float32),
                                 np.ones((128, NTH), np.float32)], 1)
        else:
            rmap = np.arange(S)
            pm = np.ones((128, NT), np.float32)
        rmap = np.ascontiguousarray(rmap.astype(np.int32).reshape(NT, 128).T)
        m = {
            "x": np.ascontiguousarray(x[b][ordr]),
            "wq": wq_p, "wkv": wkv, "wo": wo_p, "bo": bo,
            "rwt": np.ascontiguousarray(rw[exord].T),
            "w1": np.ascontiguousarray(w1[myex]),
            "b1": np.ascontiguousarray(
                b1[myex].reshape(E_MINE, MOE_HID // 128, 128).transpose(0, 2, 1)),
            "w2": np.ascontiguousarray(w2[myex]),
            "b2": np.ascontiguousarray(b2[myex].reshape(E_MINE, 1, D)),
            "cos": np.ascontiguousarray(cos2[:, ordr]),
            "sin": np.ascontiguousarray(sin2[:, ordr]),
            "coslo": np.ascontiguousarray(coslo[:, ordr]),
            "sinlo": np.ascontiguousarray(sinlo[:, ordr]),
            "rot": rotT, "idn": idn, "bde": bde, "bdeT": bdeT,
            "utri": utri, "onef": onef,
            "tokid_in": tokid,
            "xnat": x[b],
            "rowmap": rmap,
            "wq1": np.asarray(inputs["L1_attn_wq"], np.float32),
            "wk1": np.asarray(inputs["L1_attn_wk"], np.float32),
            "wv1": np.asarray(inputs["L1_attn_wv"], np.float32),
            "wo1": np.asarray(inputs["L1_attn_wo"], np.float32),
            "wg": wg, "wu": wu, "wd": wd,
            "tri": tri, "pmask": pm,
        }
        in_maps.append(m)
    return in_maps


def kernel(**inputs):
    key = "fused"
    if key not in _CACHE:
        _CACHE[key] = build_fused()
    nc, taps = _CACHE[key]
    in_maps = _prep_fused_inputs(inputs)
    res = run_bass_kernel_spmd(nc, in_maps, core_ids=list(range(8)))
    out = np.empty((B, S, D), np.float32)
    for b in range(B):
        out[b, :HALF] = res.results[2 * b]["out"]
        out[b, HALF:] = res.results[2 * b + 1]["out"]
    return out
